# version 04
# baseline (speedup 1.0000x reference)
import numpy as np
import concourse.bass as bass
import concourse.mybir as mybir
from concourse.bass_core_rust import IndirectOffsetOnAxis
from concourse.bass_utils import run_bass_kernel_spmd

F32 = mybir.dt.float32
BF16 = mybir.dt.bfloat16
U32 = mybir.dt.uint32
AF = mybir.ActivationFunctionType
ALU = mybir.AluOpType
AX = mybir.AxisListType

ENGN = ["sync", "scalar", "vector", "tensor", "gpsimd"]


class Step:
    __slots__ = ("eng", "emit", "inc", "deps", "idx", "cum", "dma")

    def __init__(self, eng, emit, inc, deps, dma):
        self.eng = eng
        self.emit = emit
        self.inc = inc
        self.deps = deps
        self.dma = dma
        self.idx = None
        self.cum = None


class Prog:
    def __init__(self, nc):
        self.nc = nc
        self.steps = []
        self.sems = {}

    def step(self, eng, emit, *, dma=False, inc=None, deps=()):
        s = Step(eng, emit, (16 if dma else 1) if inc is None else inc,
                 [d for d in deps if d is not None], dma)
        s.idx = len(self.steps)
        self.steps.append(s)
        return s

    def dma(self, eng, emit, deps=()):
        return self.step(eng, emit, dma=True, deps=deps)

    def comp(self, eng, emit, deps=()):
        return self.step(eng, emit, dma=False, deps=deps)

    def build(self, block):
        counts = {e: 0 for e in ENGN}
        for s in self.steps:
            counts[s.eng] += s.inc
            s.cum = counts[s.eng]
        sems = self.sems

        pos = {}
        cnt = {e: 0 for e in ENGN}
        for s in self.steps:
            pos[s.idx] = cnt[s.eng]
            cnt[s.eng] += 1

        def run_engine(engobj, name):
            last_wait = {e: 0 for e in ENGN}
            for s in self.steps:
                if s.eng != name:
                    continue
                if name == "vector" and any(
                        d.eng == "vector" and not d.dma and pos[d.idx] == pos[s.idx] - 1
                        for d in s.deps):
                    engobj.drain()
                for d in s.deps:
                    if d.eng == name and not d.dma:
                        continue
                    need = d.cum
                    if need > last_wait[d.eng]:
                        engobj.wait_ge(sems[d.eng], need)
                        last_wait[d.eng] = need
                ins = s.emit(engobj)
                if s.inc:
                    assert ins is not None, f"step {s.idx} on {name} returned no instruction"
                    ins.then_inc(sems[name], s.inc)

        getattr(block, "sync")(lambda e: run_engine(e, "sync"))
        getattr(block, "scalar")(lambda e: run_engine(e, "scalar"))
        getattr(block, "vector")(lambda e: run_engine(e, "vector"))
        getattr(block, "tensor")(lambda e: run_engine(e, "tensor"))
        getattr(block, "gpsimd")(lambda e: run_engine(e, "gpsimd"))


"""Full megakernel builder (one NeuronCore program, SPMD across 8 cores).

stage: "x1"  -> compute through LN1 (x1) and write X1DBG output, stop.
       "full"-> full forward, write OUT.
"""

T = 4096
D = 1024
H = 8
DK = 128
NG = 8
GT = 512
NT = 32
E = 64
CAP = 512
EPC = 8
FF = 768
SF = 768


def build(stage="x1", cores=8):
    nc = bass.Bass("TRN2", target_bir_lowering=False)
    dram = lambda n, s, d, k="ExternalInput": nc.dram_tensor(n, s, d, kind=k).ap()
    idram = lambda n, s, d: nc.dram_tensor(n, s, d, kind="Internal").ap()

    XT = dram("XT", [D, T], F32)
    XTOK = dram("XTOK", [T, D], F32)
    WQT = dram("WQT", [D, D], F32)
    WKT = dram("WKT", [D, D], F32)
    WVT = dram("WVT", [D, D], F32)
    WOT = dram("WOT", [D, D], F32)
    WG1T = dram("WG1T", [D, 128], F32)
    WG2T = dram("WG2T", [128, D], F32)
    GNW = dram("GNW", [128, 8], F32)
    RWT = dram("RWT", [D, E], F32)
    WSGU = dram("WSGU", [D, 2 * SF], F32)
    WSD = dram("WSD", [SF, D], F32)
    WGE = dram("WGE", [EPC, D, FF], F32)
    WUE = dram("WUE", [EPC, D, FF], F32)
    WDE = dram("WDE", [EPC, FF, D], F32)
    UT = dram("UT", [128, 128], F32)
    MST = dram("MST", [128, 128], F32)
    IDT = dram("IDT", [128, 128], F32)
    IOTAE = dram("IOTAE", [128, E], F32)
    TOKID = dram("TOKID", [128, NT], F32)
    PREM = dram("PREM", [8, 1], F32)
    SEL64 = dram("SEL64", [E, EPC], F32)
    EXP48 = dram("EXP48", [8, 32], F32)
    PVAL = dram("PVAL", [32, 128], F32)
    EOFF = dram("EOFF", [32, 1], F32)
    ECON = dram("ECON", [128, 32], F32)

    QPH = idram("QPH", [H, 128, T], BF16)
    EBH = idram("EBH", [H, 128, T], BF16)
    ETOKH = idram("ETOKH", [H, T, 128], BF16)
    VTOKH = idram("VTOKH", [H, T, 128], BF16)
    GSH = idram("GSH", [T, D], BF16)
    OTH = idram("OTH", [T, D], BF16)
    X1TOK = idram("X1TOK", [T, D], BF16)
    X1FMH = idram("X1FMH", [8, 128, T], BF16)
    LOCTBL = idram("LOCTBL", [CAP * E + 128 + T * E, 1], F32)
    LOCAG = nc.dram_tensor("LOCAG", [8 * (CAP * E + 128 + T * E), 1], F32, kind="Internal", addr_space="Shared").ap()
    X1AG = nc.dram_tensor("X1AG", [8 * T, D], BF16, kind="Internal", addr_space="Shared").ap()
    YB = idram("YB", [EPC * CAP, D], BF16)
    YAG = nc.dram_tensor("YAG", [8 * EPC * CAP, D], BF16, kind="Internal", addr_space="Shared").ap()

    outs = {}
    if stage == "x1":
        outs["X1DBG"] = dram("X1DBG", [T, D], F32, "ExternalOutput")
        outs["LOGDBG"] = dram("LOGDBG", [T, E], F32, "ExternalOutput")
        for nm, shp in [("DQP", [128, T]), ("DEB", [128, T]), ("DET", [GT, 128]),
                        ("DVT", [GT, 128]), ("DOT", [128, D]), ("DGS", [128, D])]:
            outs[nm] = dram(nm, shp, BF16, "ExternalOutput")
        for nm in ["DAH", "DRH", "DQS"]:
            outs[nm] = dram(nm, [128, 520], F32, "ExternalOutput")
        outs["DCA"] = dram("DCA", [128, H], F32, "ExternalOutput")
        outs["DEH"] = dram("DEH", [128, 8], F32, "ExternalOutput")
        outs["DOF"] = dram("DOF", [128, D], BF16, "ExternalOutput")
        outs["DX1P"] = dram("DX1P", [128, D], F32, "ExternalOutput")
        outs["DCOL"] = dram("DCOL", [128, 8], F32, "ExternalOutput")
    else:
        outs["OUT"] = dram("OUT", [T, D], F32, "ExternalOutput")
        if stage == "full_dbg":
            outs["DLOC"] = dram("DLOC", [CAP * E + 128, 1], F32, "ExternalOutput")
            outs["DGL"] = dram("DGL", [128, 32], F32, "ExternalOutput")
            outs["DWC"] = dram("DWC", [128, 32], F32, "ExternalOutput")
            outs["DXI"] = dram("DXI", [128, 32], U32, "ExternalOutput")
            outs["DYB"] = dram("DYB", [256, D], BF16, "ExternalOutput")
            outs["DMSK"] = dram("DMSK", [128, NT * E], F32, "ExternalOutput")
            outs["DCNT"] = dram("DCNT", [128, NT * E], F32, "ExternalOutput")
            outs["DXB"] = dram("DXB", [128, 4096], BF16, "ExternalOutput")
            outs["DXBF"] = dram("DXBF", [128, 4096], BF16, "ExternalOutput")
            outs["DGSB"] = dram("DGSB", [128, 3072], BF16, "ExternalOutput")
            outs["DHSB"] = dram("DHSB", [128, 3072], BF16, "ExternalOutput")
            outs["DYP"] = dram("DYP", [128, 512], F32, "ExternalOutput")

    P = Prog(nc)
    sbA = nc.alloc_sbuf_tensor

    class Region:
        def __init__(self):
            self.start = None
            self.cur = None
        def init(self):
            self.start = nc.bytes_allocated(None) if False else None
        def alloc(self, name, shape, dtype):
            import math
            eb = mybir.dt.size(dtype)
            n = 1
            for s_ in shape[1:]:
                n *= s_
            nb = n * eb
            off = (self.cur + 31) // 32 * 32
            self.cur = off + nb
            assert self.cur <= 229300, f"SBUF overlay overflow at {name}: {self.cur}"
            return nc.alloc_sbuf_tensor_at(name, shape, dtype, offset=off)
        def mark(self):
            return self.cur
        def reset(self, m):
            self.cur = m
    REG = Region()

    cU = sbA("cU", [128, 128], F32)
    cIDF = sbA("cIDF", [128, 128], F32)
    cMST = sbA("cMST", [128, 128], BF16)
    cID = sbA("cID", [128, 128], BF16)
    cIOTAE = sbA("cIOTAE", [128, E], F32)
    cTOKID = sbA("cTOKID", [128, NT], F32)
    cPREM = sbA("cPREM", [8, 1], F32)
    cSEL = sbA("cSEL", [E, EPC], F32)
    cEXP48 = sbA("cEXP48", [8, 32], F32)
    cPVAL = sbA("cPVAL", [32, 128], F32)
    cEOFF = sbA("cEOFF", [32, 1], F32)
    cECON = sbA("cECON", [128, 32], F32)
    cONEROW = sbA("cONEROW", [1, 128], F32)
    cONECOL = sbA("cONECOL", [128, 1], F32)
    cZ = sbA("cZ", [128, 544], F32)
    MSK = sbA("MSK", [128, NT * E], F32)
    CNT = sbA("CNT", [128, NT * E], F32)
    CARRY = sbA("CARRY", [128, 8 * H], F32)
    TPROW = sbA("TPROW", [1, E], F32)
    BASER = sbA("BASER", [1, E], F32)
    BASEBC = sbA("BASEBC", [128, E], F32)

    from concourse import bass_primitives as _bp
    REG.cur = int(nc.bytes_allocated(_bp.MemorySpace.SBUF))
    sbR = REG.alloc

    p = [nc.alloc_psum_tensor(f"p{i}", [128, 512], F32) for i in range(6)]
    pt6 = nc.alloc_psum_tensor("pt6", [128, 1024], BF16)
    pt7 = nc.alloc_psum_tensor("pt7", [128, 1024], BF16)

    sems = {}
    semctx = []
    for e in ["sync", "scalar", "vector", "tensor", "gpsimd", "cc"]:
        h = nc.alloc_semaphore(f"sem_{e}")
        sems[e] = h
    P.sems = sems

    S = P.step
    DMA = P.dma
    C = P.comp

    rB32K = nc.gpsimd.alloc_register("bnd32k")
    rB256K = nc.gpsimd.alloc_register("bnd256k")
    rB2M = nc.gpsimd.alloc_register("bnd2m")

    S("gpsimd", lambda e: e.reg_mov(rB32K, CAP * E - 1), inc=0)
    S("gpsimd", lambda e: e.reg_mov(rB256K, 8 * (CAP * E + 128 + T * E) - 1), inc=0)
    S("gpsimd", lambda e: e.reg_mov(rB2M, 8 * (CAP * E + 128 + T * E) - 1), inc=0)
    ld_consts = []
    for hbm, tile in [(UT, cU), (IDT, cIDF), (IOTAE, cIOTAE), (TOKID, cTOKID), (PREM, cPREM),
                      (SEL64, cSEL), (EXP48, cEXP48), (PVAL, cPVAL), (EOFF, cEOFF),
                      (ECON, cECON)]:
        ld_consts.append(DMA("sync", lambda e, a=hbm, b=tile: e.dma_start(out=b[:], in_=a[:])))
    ld_cmst = DMA("gpsimd", lambda e: e.dma_start(out=cMST[:], in_=MST[:]))
    ld_cid = DMA("gpsimd", lambda e: e.dma_start(out=cID[:], in_=IDT[:]))
    c_ones = C("vector", lambda e: e.memset(cONEROW[:], 1.0))
    c_ones2 = C("vector", lambda e: e.memset(cONECOL[:], 1.0))
    c_z = C("vector", lambda e: e.memset(cZ[:], 0.0))
    c_carry = C("vector", lambda e: e.memset(CARRY[:], 0.0))
    c_tprow = C("vector", lambda e: e.memset(TPROW[:], 0.0))
    z_loc = DMA("sync", lambda e: e.dma_start(
        out=LOCTBL[0:CAP * E + 128, 0:1].rearrange("(a b) c -> a (b c)", a=128),
        in_=cZ[:, 0:(CAP * E + 128) // 128]), deps=[c_z])

    wq = sbR("wq", [128, 8 * 1024], BF16)
    wk = sbR("wk", [128, 8 * 1024], BF16)
    wv = sbR("wv", [128, 8 * 1024], BF16)
    wg1 = sbR("wg1", [128, 8 * 128], BF16)
    wg2 = sbR("wg2", [128, 1024], BF16)

    def ld_w(hbm, tile, ncol):
        return DMA("gpsimd", lambda e: e.dma_start(
            out=tile[:].rearrange("p (c n) -> p c n", n=ncol),
            in_=hbm[:].rearrange("(c p) n -> p c n", p=128)))
    ld_wq = ld_w(WQT, wq, 1024)
    ld_wk = ld_w(WKT, wk, 1024)
    ld_wv = ld_w(WVT, wv, 1024)
    ld_wg1 = ld_w(WG1T, wg1, 128)
    ld_wg2 = DMA("gpsimd", lambda e: e.dma_start(out=wg2[:], in_=WG2T[:]))

    A1_BASE = REG.cur
    xfm = sbR("xfm", [128, 8 * GT], BF16)
    EHb = sbR("EHb", [128, 520], F32)
    AHb = sbR("AHb", [128, 520], F32)
    RHb = sbR("RHb", [128, GT], F32)
    QSIL = sbR("QSIL", [128, GT], F32)
    QPg = sbR("QPg", [128, H * GT], BF16)
    EBg = sbR("EBg", [128, H * GT], BF16)
    VTOKg = sbR("VTOKg", [128, 4 * 1024], BF16)
    ETOKg = sbR("ETOKg", [128, 4 * 1024], BF16)
    G1b = sbR("G1b", [128, GT], BF16)
    GSg = sbR("GSg", [128, 4 * 1024], BF16)

    last_wr = {"QP": None, "EB": None, "VT": None, "ET": None, "GS": None}
    prev_q_mm = None
    prev_scan = None
    prev_recip = None
    prev_veb = None
    prev_vqp = None
    last_xfm_reader = None
    prev_probe = None

    for g in range(NG):
        t0 = g * GT
        ld_x = DMA("gpsimd", lambda e, t0=t0: e.dma_start(
            out=xfm[:].rearrange("p (c t) -> p c t", t=GT),
            in_=XT[:, t0:t0 + GT].rearrange("(c p) t -> p c t", p=128)),
            deps=[last_wr["QP"], last_wr["EB"], last_xfm_reader])
        head_deps = []
        ebg_done = []
        qpg_done = []
        war_ebg = last_wr["EB_rd"] if "EB_rd" in last_wr else None
        war_qpg = last_wr["QP"]
        for h in range(H):
            def k_mm(e, h=h):
                for dc in range(8):
                    ins = e.matmul(p[0][:, 0:GT], wk[:, dc * 1024 + h * 128: dc * 1024 + (h + 1) * 128],
                                   xfm[:, dc * GT:(dc + 1) * GT], start=(dc == 0), stop=(dc == 7))
                return ins
            mm_k = C("tensor", k_mm, deps=[ld_x, ld_wk, prev_scan])
            s_exp = C("scalar", lambda e: e.activation(out=EHb[:, 1:1 + GT], in_=p[0][:, 0:GT], func=AF.Exp),
                      deps=[mm_k, prev_scan, prev_veb, prev_recip])
            v_eb = C("vector", lambda e, h=h: e.tensor_copy(out=EBg[:, h * GT:(h + 1) * GT], in_=EHb[:, 1:1 + GT]),
                     deps=[s_exp, war_ebg])
            prev_veb = v_eb
            ebg_done.append(v_eb)
            v_scan = C("vector", lambda e: e.tensor_tensor_scan(
                AHb[:, 1:1 + GT], EHb[:, 1:1 + GT], cZ[:, 0:GT], 0.0, ALU.add, ALU.add),
                deps=[s_exp, c_z, prev_probe])
            prev_scan = v_scan
            v_add = C("vector", lambda e, h=h: e.tensor_tensor(
                EHb[:, 1:1 + GT], AHb[:, 1:1 + GT],
                CARRY[:, 8 * h:8 * h + 1].broadcast_to([128, GT]), ALU.add),
                deps=[v_scan, v_eb, c_carry])
            v_sc = C("vector", lambda e, h=h: e.tensor_copy(out=CARRY[:, 8 * h:8 * h + 1], in_=EHb[:, GT:GT + 1]),
                     deps=[v_add, prev_probe])
            v_r = C("vector", lambda e: e.reciprocal(RHb[:], EHb[:, 1:1 + GT]),
                    deps=[v_sc, prev_vqp, prev_probe])
            prev_recip = v_r
            if stage == "x1" and g == 1 and h == 0:
                dbc = DMA("sync", lambda e: e.dma_start(out=outs["DCA"][:], in_=CARRY[:, 0:8]), deps=[v_add])
                dbE = DMA("sync", lambda e: e.dma_start(out=outs["DEH"][:], in_=EHb[:, 0:8]), deps=[v_add])
                db1 = DMA("sync", lambda e: e.dma_start(out=outs["DAH"][:], in_=EHb[:]), deps=[v_add, v_r])
                prev_probe = db1
            def q_mm(e, h=h):
                for dc in range(8):
                    ins = e.matmul(p[1][:, 0:GT], wq[:, dc * 1024 + h * 128: dc * 1024 + (h + 1) * 128],
                                   xfm[:, dc * GT:(dc + 1) * GT], start=(dc == 0), stop=(dc == 7))
                return ins
            mm_q = C("tensor", q_mm, deps=[ld_x, ld_wq, prev_q_mm])
            s_sil = C("scalar", lambda e: e.activation(out=QSIL[:], in_=p[1][:, 0:GT], func=AF.Silu),
                      deps=[mm_q, prev_vqp, prev_probe])
            prev_q_mm = s_sil
            v_qp = C("vector", lambda e, h=h: e.tensor_tensor(
                QPg[:, h * GT:(h + 1) * GT], QSIL[:], RHb[:], ALU.mult), deps=[s_sil, v_r, war_qpg])
            if stage == "x1" and g == 1 and h == 0:
                db2 = DMA("sync", lambda e: e.dma_start(out=outs["DRH"][:, 0:GT], in_=RHb[:]), deps=[v_r, v_qp])
                db3 = DMA("sync", lambda e: e.dma_start(out=outs["DQS"][:, 0:GT], in_=QSIL[:]), deps=[s_sil, v_qp])
                prev_probe = db3
            prev_vqp = v_qp
            qpg_done.append(v_qp)
            head_deps.append(v_qp)
        wr_qp = DMA("sync", lambda e, t0=t0: e.dma_start(
            out=QPH[:, :, t0:t0 + GT].rearrange("h p t -> p h t"), in_=QPg[:].rearrange("p (h t) -> p h t", t=GT)),
            deps=qpg_done)
        wr_eb = DMA("sync", lambda e, t0=t0: e.dma_start(
            out=EBH[:, :, t0:t0 + GT].rearrange("h p t -> p h t"), in_=EBg[:].rearrange("p (h t) -> p h t", t=GT)),
            deps=ebg_done)
        last_wr["QP"] = wr_qp
        last_wr["EB"] = wr_eb

        vt_done = []
        et_done = []
        for i in range(4):
            def v_mm(e, i=i, half=0):
                for dc in range(8):
                    ins = e.matmul(p[2][:, 0:512],
                                   xfm[:, dc * GT + i * 128: dc * GT + i * 128 + 128],
                                   wv[:, dc * 1024: dc * 1024 + 512],
                                   start=(dc == 0), stop=(dc == 7))
                return ins
            def v_mm2(e, i=i):
                for dc in range(8):
                    ins = e.matmul(p[3][:, 0:512],
                                   xfm[:, dc * GT + i * 128: dc * GT + i * 128 + 128],
                                   wv[:, dc * 1024 + 512: (dc + 1) * 1024],
                                   start=(dc == 0), stop=(dc == 7))
                return ins
            dep_war = vt_done[-2] if len(vt_done) >= 2 else last_wr["VT"]
            mm_v1 = C("tensor", v_mm, deps=[ld_x, ld_wv, dep_war])
            mm_v2 = C("tensor", v_mm2, deps=[ld_x, ld_wv, dep_war])
            ev_v1 = C("vector", lambda e, i=i: e.tensor_copy(
                out=VTOKg[:, i * 1024: i * 1024 + 512], in_=p[2][:, 0:512]), deps=[mm_v1])
            ev_v2 = C("vector", lambda e, i=i: e.tensor_copy(
                out=VTOKg[:, i * 1024 + 512: (i + 1) * 1024], in_=p[3][:, 0:512]), deps=[mm_v2])
            vt_done += [ev_v1, ev_v2]
            for h in range(H):
                pt = pt6 if h % 2 == 0 else pt7
                dep_tr_war = et_done[-2] if len(et_done) >= 2 else None
                tr = C("tensor", lambda e, i=i, h=h, pt=pt: e.transpose(
                    pt[:, 0:128], EBg[:, h * GT + i * 128: h * GT + i * 128 + 128], cID[:]),
                    deps=[ebg_done[h], ld_cid, dep_tr_war])
                ev = C("vector", lambda e, i=i, h=h, pt=pt: e.tensor_copy(
                    out=ETOKg[:, i * 1024 + h * 128: i * 1024 + (h + 1) * 128], in_=pt[:, 0:128]),
                    deps=[tr, last_wr["ET"] if (i == 0 and h == 0) else None])
                et_done.append(ev)
        for h in range(H):
            wr_vt = DMA("sync", lambda e, t0=t0, h=h: e.dma_start(
                out=VTOKH[h, t0:t0 + GT, :].rearrange("(i p) c -> p i c", p=128),
                in_=VTOKg[:].rearrange("p (i hc) -> p i hc", i=4)[:, :, h * 128:(h + 1) * 128]),
                deps=vt_done if h == 0 else ())
            wr_et = DMA("sync", lambda e, t0=t0, h=h: e.dma_start(
                out=ETOKH[h, t0:t0 + GT, :].rearrange("(i p) c -> p i c", p=128),
                in_=ETOKg[:].rearrange("p (i hc) -> p i hc", i=4)[:, :, h * 128:(h + 1) * 128]),
                deps=et_done if h == 0 else ())
        last_wr["VT"] = wr_vt
        last_wr["ET"] = wr_et
        last_wr["EB_rd"] = wr_et

        def g1_mm(e):
            for dc in range(8):
                ins = e.matmul(p[4][:, 0:GT], wg1[:, dc * 128: (dc + 1) * 128],
                               xfm[:, dc * GT:(dc + 1) * GT], start=(dc == 0), stop=(dc == 7))
            return ins
        mm_g1 = C("tensor", g1_mm, deps=[ld_x, ld_wg1, last_wr["GS_g1war"] if "GS_g1war" in last_wr else None])
        last_xfm_reader = mm_g1
        ev_g1 = C("vector", lambda e: e.tensor_copy(out=G1b[:], in_=p[4][:, 0:GT]), deps=[mm_g1])
        gs_done = []
        for i in range(4):
            mm_ga = C("tensor", lambda e, i=i: e.matmul(
                p[5][:, 0:512], G1b[:, i * 128:(i + 1) * 128], wg2[:, 0:512], start=True, stop=True),
                deps=[ev_g1, ld_wg2, gs_done[-2] if len(gs_done) >= 2 else None])
            mm_gb = C("tensor", lambda e, i=i: e.matmul(
                p[4][:, 0:512], G1b[:, i * 128:(i + 1) * 128], wg2[:, 512:1024], start=True, stop=True),
                deps=[ev_g1, ld_wg2, gs_done[-1] if len(gs_done) >= 1 else None])
            sa = C("scalar", lambda e, i=i: e.activation(
                out=GSg[:, i * 1024: i * 1024 + 512], in_=p[5][:, 0:512], func=AF.Silu), deps=[mm_ga])
            sb_ = C("scalar", lambda e, i=i: e.activation(
                out=GSg[:, i * 1024 + 512: (i + 1) * 1024], in_=p[4][:, 0:512], func=AF.Silu), deps=[mm_gb])
            gs_done += [sa, sb_]
        wr_gs = DMA("sync", lambda e, t0=t0: e.dma_start(
            out=GSH[t0:t0 + GT, :].rearrange("(i p) c -> p i c", p=128),
            in_=GSg[:].rearrange("p (i c) -> p i c", i=4)), deps=gs_done)
        last_wr["GS"] = wr_gs
        last_wr["GS_g1war"] = gs_done[-1]

    a1_done = [last_wr[k] for k in last_wr]

    A1_MARK = REG.mark()
    REG.reset(A1_BASE)
    QPs = [sbR(f"QPs{k}", [128, T], BF16) for k in range(2)]
    EBs = [sbR(f"EBs{k}", [128, T], BF16) for k in range(2)]
    ETs = [sbR(f"ETs{k}", [128, T], BF16) for k in range(2)]
    VTs = [sbR(f"VTs{k}", [128, T], BF16) for k in range(2)]
    PMT = [sbR(f"PMT{k}", [128, 128], BF16) for k in range(4)]
    SBF = [sbR(f"SBF{k}", [128, 128], BF16) for k in range(2)]
    OST = [sbR(f"OST{k}", [128, 128], BF16) for k in range(4)]

    ot_wr = []
    prev_pair_state = []
    for pair in range(4):
        lds = []
        for k in range(2):
            h = 2 * pair + k
            lds.append((
                DMA("sync", lambda e, h=h, k=k: e.dma_start(out=QPs[k][:], in_=QPH[h]), deps=[last_wr["QP"]] + prev_pair_state),
                DMA("sync", lambda e, h=h, k=k: e.dma_start(out=EBs[k][:], in_=EBH[h]), deps=[last_wr["EB"]] + prev_pair_state),
                DMA("sync", lambda e, h=h, k=k: e.dma_start(
                    out=ETs[k][:].rearrange("p (i c) -> p i c", c=128),
                    in_=ETOKH[h].rearrange("(i p) c -> p i c", p=128)), deps=[last_wr["ET"]] + prev_pair_state),
                DMA("sync", lambda e, h=h, k=k: e.dma_start(
                    out=VTs[k][:].rearrange("p (i c) -> p i c", c=128),
                    in_=VTOKH[h].rearrange("(i p) c -> p i c", p=128)), deps=[last_wr["VT"]] + prev_pair_state),
            ))
        state = [None, None]
        o_mm_prev = [None, None]
        pmt_prev = {}
        ost_prev = {}
        for c in range(NT):
            for k in range(2):
                h = 2 * pair + k
                pS = p[4] if k == 0 else p[5]
                pPT = p[2] if (c % 2 == 0) else p[3]
                pO = p[0] if (c % 2 == 0) else p[1]
                pmt = PMT[2 * k + c % 2]
                ost = OST[2 * k + c % 2]
                sl = slice(c * 128, (c + 1) * 128)
                mm_pt = C("tensor", lambda e, k=k, sl=sl, pPT=pPT: e.matmul(
                    pPT[:, 0:128], EBs[k][:, sl], QPs[k][:, sl], start=True, stop=True),
                    deps=list(lds[k][:2]) + [pmt_prev.get((k, c % 2))])
                ev_pt = C("vector", lambda e, pmt=pmt, pPT=pPT: e.tensor_tensor(
                    pmt[:], pPT[:, 0:128], cMST[:], ALU.mult), deps=[mm_pt])
                pmt_prev[(k, c % 2)] = ev_pt

                def o_mms(e, k=k, sl=sl, pO=pO, pmt=pmt, c=c):
                    if c > 0:
                        e.matmul(pO[:, 0:128], QPs[k][:, sl], SBF[k][:], start=True, stop=False)
                    ins = e.matmul(pO[:, 0:128], pmt[:], VTs[k][:, sl], start=(c == 0), stop=True)
                    return ins
                mm_o = C("tensor", o_mms,
                         deps=[ev_pt, state[k], lds[k][3], ost_prev.get((k, c % 2))])
                mm_s = C("tensor", lambda e, k=k, sl=sl, pS=pS, c=c: e.matmul(
                    pS[:, 0:128], ETs[k][:, sl], VTs[k][:, sl], start=(c == 0), stop=(c == NT - 1)),
                    deps=[lds[k][2], lds[k][3], mm_o])
                ev_s = C("vector", lambda e, k=k, pS=pS: e.tensor_copy(out=SBF[k][:], in_=pS[:, 0:128]),
                         deps=[mm_s])
                state[k] = ev_s
                ev_o = C("vector", lambda e, ost=ost, pO=pO: e.tensor_copy(out=ost[:], in_=pO[:, 0:128]),
                         deps=[mm_o])
                wr_o = DMA("sync", lambda e, c=c, h=h, ost=ost: e.dma_start(
                    out=OTH[c * 128:(c + 1) * 128, h * 128:(h + 1) * 128], in_=ost[:]), deps=[ev_o])
                ost_prev[(k, c % 2)] = wr_o
                ot_wr.append(wr_o)
        if c == NT - 1:
            prev_pair_state = [state[0], state[1]] + ot_wr[-2:]

    a2_final = prev_pair_state + ot_wr[-1:]
    REG.reset(A1_BASE)
    wo = sbR("wo", [128, 8 * 1024], BF16)
    rw = sbR("rw", [128, 8 * E], BF16)
    gnwc = sbR("gnwc", [128, 8], F32)
    wof = sbR("wof", [128, 1024], F32)
    ld_gnw = DMA("sync", lambda e: e.dma_start(out=gnwc[:], in_=GNW[:]), deps=a2_final)
    ld_rw = DMA("gpsimd", lambda e: e.dma_start(
        out=rw[:].rearrange("p (c n) -> p c n", n=E),
        in_=RWT[:].rearrange("(c p) n -> p c n", p=128)), deps=a2_final)
    wo_ready = []
    ld_wo_f32 = []
    for dc in range(8):
        ldc = DMA("sync", lambda e, dc=dc: e.dma_start(out=wof[:], in_=WOT[dc * 128:(dc + 1) * 128, :]),
                  deps=[ld_gnw, wo_ready[-1] if wo_ready else None] + a2_final)
        sc = C("vector", lambda e, dc=dc: e.tensor_scalar_mul(
            wo[:, dc * 1024:(dc + 1) * 1024], wof[:], gnwc[:, dc:dc + 1]), deps=[ldc, ld_gnw])
        wo_ready.append(sc)

    OTt = sbR("OTt", [128, 1024], BF16)
    GSt = sbR("GSt", [128, 1024], BF16)
    XTt = sbR("XTt", [128, 1024], F32)
    SQ = sbR("SQ", [128, 1024], F32)
    COLS = sbR("COLS", [128, 8], F32)
    OF = sbR("OF", [128, 1024], BF16)
    OFM = sbR("OFM", [128, 1024], BF16)
    X1P = sbR("X1P", [128, 1024], F32)
    X1B = sbR("X1B", [128, 1024], BF16)
    X1F = sbR("X1F", [128, 1024], BF16)
    LOGb = sbR("LOGb", [128, E], F32)
    ELb = sbR("ELb", [128, E], F32)
    T8b = sbR("T8b", [128, 8], F32)
    WNb = sbR("WNb", [128, E], F32)
    Wtb = sbR("Wtb", [128, E], F32)
    S8b = sbR("S8b", [128, 8], F32)
    S8f = sbR("S8f", [128, 8], F32)
    S8u = sbR("S8u", [128, 8], U32)
    X1DBGs = sbR("X1DBGs", [128, 1024], F32) if stage == "x1" else None

    tp_prev = None
    prev_tile = {}
    for i in range(NT):
        t0 = i * 128
        dp = prev_tile
        first_deps = a2_final if i == 0 else []
        ld_ot = DMA("sync", lambda e, t0=t0: e.dma_start(out=OTt[:], in_=OTH[t0:t0 + 128, :]),
                    deps=(ot_wr[-1:] + a2_final) if i == 0 else [dp.get("otwar")])
        ld_gs = DMA("sync", lambda e, t0=t0: e.dma_start(out=GSt[:], in_=GSH[t0:t0 + 128, :]),
                    deps=[last_wr["GS"], dp.get("otwar")] + first_deps)
        ld_xt = DMA("sync", lambda e, t0=t0: e.dma_start(out=XTt[:], in_=XTOK[t0:t0 + 128, :]),
                    deps=[dp.get("x1p")] + first_deps)
        s_sq = C("scalar", lambda e: e.activation(out=SQ[:], in_=OTt[:], func=AF.Square),
                 deps=[ld_ot, dp.get("sq_done")])
        v_sq = C("vector", lambda e: e.reduce_sum(COLS[:, 0:1], SQ[:], axis=AX.X), deps=[s_sq])
        v_rc = C("vector", lambda e: e.tensor_scalar(COLS[:, 1:2], COLS[:, 0:1], 1.0 / D, 1e-5 * DK,
                                                     ALU.mult, ALU.add), deps=[v_sq])
        s_rc = C("scalar", lambda e: e.activation(out=COLS[:, 2:3], in_=COLS[:, 1:2], func=AF.Sqrt),
                 deps=[v_rc])
        v_rc2 = C("vector", lambda e: e.reciprocal(COLS[:, 3:4], COLS[:, 2:3]), deps=[s_rc])
        v_of = C("vector", lambda e: e.scalar_tensor_tensor(
            OF[:], OTt[:], COLS[:, 3:4], GSt[:], ALU.mult, ALU.mult), deps=[v_rc2, ld_gs])
        prev_tile_of = v_of
        if stage == "x1" and i == 4:
            DMA("sync", lambda e: e.dma_start(out=outs["DOF"][:], in_=OF[:]), deps=[v_of])
        trs = []
        for h in range(8):
            pt = pt6 if h % 2 == 0 else pt7
            tr = C("tensor", lambda e, h=h, pt=pt: e.transpose(
                pt[:, 0:128], OF[:, h * 128:(h + 1) * 128], cID[:]),
                deps=[v_of, ld_cid, trs[-2][1] if len(trs) >= 2 else dp.get("womm")])
            ev = C("vector", lambda e, h=h, pt=pt: e.tensor_copy(
                out=OFM[:, h * 128:(h + 1) * 128], in_=pt[:, 0:128]), deps=[tr])
            trs.append((tr, ev))
        def wo_mm(e, half):
            for cc in range(8):
                ins = e.matmul(p[0 + half][:, 0:512], OFM[:, cc * 128:(cc + 1) * 128],
                               wo[:, cc * 1024 + half * 512: cc * 1024 + half * 512 + 512],
                               start=(cc == 0), stop=(cc == 7))
            return ins
        mm_wo0 = C("tensor", lambda e: wo_mm(e, 0), deps=[ev for _, ev in trs] + wo_ready + [dp.get("x1p")])
        mm_wo1 = C("tensor", lambda e: wo_mm(e, 1), deps=[mm_wo0])
        v_x1a = C("vector", lambda e: e.tensor_tensor(X1P[:, 0:512], p[0][:, 0:512], XTt[:, 0:512], ALU.add),
                  deps=[mm_wo0, ld_xt])
        v_x1b = C("vector", lambda e: e.tensor_tensor(X1P[:, 512:1024], p[1][:, 0:512], XTt[:, 512:1024], ALU.add),
                  deps=[mm_wo1, ld_xt])
        v_nm = C("vector", lambda e: e.tensor_reduce(COLS[:, 4:5], X1P[:], AX.X, ALU.add, negate=True),
                 deps=[v_x1a, v_x1b])
        v_nm2 = C("vector", lambda e: e.tensor_scalar_mul(COLS[:, 4:5], COLS[:, 4:5], 1.0 / D), deps=[v_nm])
        s_sq2 = C("scalar", lambda e: e.activation(out=SQ[:], in_=X1P[:], func=AF.Square),
                  deps=[v_x1a, v_x1b, s_sq, v_sq])
        v_sq2 = C("vector", lambda e: e.reduce_sum(COLS[:, 5:6], SQ[:], axis=AX.X), deps=[s_sq2])
        sq_done = v_sq2
        v_m2 = C("vector", lambda e: e.tensor_tensor(COLS[:, 6:7], COLS[:, 4:5], COLS[:, 4:5], ALU.mult),
                 deps=[v_nm2])
        v_var = C("vector", lambda e: e.tensor_scalar(COLS[:, 5:6], COLS[:, 5:6], 1.0 / D, 1e-5,
                                                      ALU.mult, ALU.add), deps=[v_sq2])
        v_var2 = C("vector", lambda e: e.tensor_sub(COLS[:, 5:6], COLS[:, 5:6], COLS[:, 6:7]),
                   deps=[v_var, v_m2])
        s_sd = C("scalar", lambda e: e.activation(out=COLS[:, 6:7], in_=COLS[:, 5:6], func=AF.Sqrt),
                 deps=[v_var2])
        v_rsd = C("vector", lambda e: e.reciprocal(COLS[:, 7:8], COLS[:, 6:7]), deps=[s_sd])
        if stage == "x1" and i == 4:
            DMA("sync", lambda e: e.dma_start(out=outs["DX1P"][:], in_=X1P[:]), deps=[v_x1a, v_x1b, v_rsd])
            DMA("sync", lambda e: e.dma_start(out=outs["DCOL"][:], in_=COLS[:]), deps=[v_rsd])
        v_x1 = C("vector", lambda e: e.scalar_tensor_tensor(
            X1B[:], X1P[:], COLS[:, 4:5], COLS[:, 7:8].broadcast_to([128, 1024]), ALU.add, ALU.mult),
            deps=[v_rsd, dp.get("x1war"), dp.get("x1wr")])
        if stage == "x1":
            v_x1f = C("vector", lambda e: e.scalar_tensor_tensor(
                X1DBGs[:], X1P[:], COLS[:, 4:5], COLS[:, 7:8].broadcast_to([128, 1024]), ALU.add, ALU.mult),
                deps=[v_rsd])
            DMA("sync", lambda e, t0=t0: e.dma_start(out=outs["X1DBG"][t0:t0 + 128, :], in_=X1DBGs[:]),
                deps=[v_x1f])
        wr_x1 = DMA("sync", lambda e, t0=t0: e.dma_start(out=X1TOK[t0:t0 + 128, :], in_=X1B[:]), deps=[v_x1])
        prev_x1wr = wr_x1
        trs2 = []
        for dc in range(8):
            pt = pt6 if dc % 2 == 0 else pt7
            tr = C("tensor", lambda e, dc=dc, pt=pt: e.transpose(
                pt[:, 0:128], X1B[:, dc * 128:(dc + 1) * 128], cID[:]),
                deps=[v_x1, ld_cid, trs2[-2][1] if len(trs2) >= 2 else (trs[-1][0] if trs else None),
                      dp.get("x1fwr") if dc == 0 else None])
            ev = C("vector", lambda e, dc=dc, pt=pt: e.tensor_copy(
                out=X1F[:, dc * 128:(dc + 1) * 128], in_=pt[:, 0:128]), deps=[tr])
            trs2.append((tr, ev))
        wr_x1f = DMA("sync", lambda e, t0=t0: e.dma_start(
            out=X1FMH[:, :, t0:t0 + 128].rearrange("c p t -> p c t"), in_=X1F[:].rearrange("p (c t) -> p c t", t=128)),
            deps=[ev for _, ev in trs2])
        def r_mm(e):
            for dc in range(8):
                ins = e.matmul(p[2][:, 0:E], X1F[:, dc * 128:(dc + 1) * 128], rw[:, dc * E:(dc + 1) * E],
                               start=(dc == 0), stop=(dc == 7))
            return ins
        mm_r = C("tensor", r_mm, deps=[ev for _, ev in trs2] + [ld_rw, dp.get("log")])
        v_log = C("vector", lambda e: e.tensor_copy(out=LOGb[:], in_=p[2][:, 0:E]), deps=[mm_r])
        if stage == "x1":
            DMA("sync", lambda e, t0=t0: e.dma_start(out=outs["LOGDBG"][t0:t0 + 128, :], in_=LOGb[:]),
                deps=[v_log])
        v_mx = C("vector", lambda e: e.tensor_reduce(COLS[:, 0:1], LOGb[:], AX.X, ALU.max, negate=True),
                 deps=[v_log, s_sq])
        s_el = C("scalar", lambda e: e.activation(out=ELb[:], in_=LOGb[:], func=AF.Exp,
                                                  bias=COLS[:, 0:1]), deps=[v_mx])
        v_t8 = C("vector", lambda e: e.max(T8b[:], LOGb[:]), deps=[v_log])
        v_ms = C("vector", lambda e, i=i: e.tensor_single_scalar(
            MSK[:, i * E:(i + 1) * E], LOGb[:], T8b[:, 7:8], ALU.is_ge), deps=[v_t8])
        v_wn = C("vector", lambda e, i=i: e.tensor_tensor(WNb[:], ELb[:], MSK[:, i * E:(i + 1) * E], ALU.mult),
                 deps=[s_el, v_ms])
        v_s = C("vector", lambda e: e.tensor_reduce(COLS[:, 1:2], WNb[:], AX.X, ALU.add), deps=[v_wn])
        v_sr = C("vector", lambda e: e.reciprocal(COLS[:, 2:3], COLS[:, 1:2]), deps=[v_s])
        v_w = C("vector", lambda e: e.tensor_scalar_mul(Wtb[:], WNb[:], COLS[:, 2:3]), deps=[v_sr, dp.get("wwar")])
        wr_w = DMA("sync", lambda e, t0=t0: e.dma_start(
            out=LOCTBL[CAP * E + 128 + t0 * E: CAP * E + 128 + (t0 + 128) * E, 0:1]
                .rearrange("(p e) c -> p (e c)", e=E), in_=Wtb[:]), deps=[v_w])
        mm_cnt = C("tensor", lambda e, i=i: (
            e.matmul(p[3][:, 0:E], cU[:], MSK[:, i * E:(i + 1) * E], start=True, stop=False),
            e.matmul(p[3][:, 0:E], cONEROW[:], TPROW[:], start=False, stop=True))[-1],
            deps=[v_ms, c_ones, tp_prev or c_tprow, dp.get("cnt_ev")])
        v_cnt = C("vector", lambda e, i=i: e.tensor_tensor(
            CNT[:, i * E:(i + 1) * E], p[3][:, 0:E], MSK[:, i * E:(i + 1) * E], ALU.subtract),
            deps=[mm_cnt])
        mm_tp = C("tensor", lambda e, i=i: e.matmul(
            p[5][0:1, 0:E], cONECOL[:], MSK[:, i * E:(i + 1) * E], start=(i == 0), stop=(i == NT - 1)),
            deps=[v_ms, c_ones2, mm_cnt])
        v_tp = C("vector", lambda e: e.tensor_copy(out=TPROW[:], in_=p[5][0:1, 0:E]), deps=[mm_tp])
        tp_prev = v_tp
        v_v1 = C("vector", lambda e, i=i: e.tensor_single_scalar(WNb[:], CNT[:, i * E:(i + 1) * E],
                                                                 511.5, ALU.is_lt), deps=[v_cnt, v_w])
        v_v2 = C("vector", lambda e, i=i: e.tensor_tensor(WNb[:], WNb[:], MSK[:, i * E:(i + 1) * E],
                                                          ALU.mult), deps=[v_v1])
        v_t1 = C("vector", lambda e, i=i: e.tensor_tensor(ELb[:], CNT[:, i * E:(i + 1) * E], cIOTAE[:],
                                                          ALU.add), deps=[v_cnt, s_el])
        v_t2 = C("vector", lambda e: e.tensor_tensor(ELb[:], ELb[:], WNb[:], ALU.mult), deps=[v_t1, v_v2])
        v_sx = C("vector", lambda e: e.tensor_scalar_add(ELb[:], ELb[:], -1.0), deps=[v_t2])
        v_m8 = C("vector", lambda e: e.max(S8b[:], ELb[:]), deps=[v_sx])
        v_oo = C("vector", lambda e: e.tensor_single_scalar(S8f[:], S8b[:], 0.0, ALU.is_lt), deps=[v_m8])
        v_fx = C("vector", lambda e: e.scalar_tensor_tensor(S8f[:], S8f[:], 2.0e9, S8b[:],
                                                            ALU.mult, ALU.add), deps=[v_oo])
        v_cu = C("vector", lambda e: e.tensor_copy(out=S8u[:], in_=S8f[:]), deps=[v_fx, dp.get("s8war")])
        sc_last = None
        for j in range(8):
            sc_last = DMA("gpsimd", lambda e, j=j, i=i: e.indirect_dma_start(
                out=LOCTBL[:, :], out_offset=IndirectOffsetOnAxis(S8u[:, j:j + 1], 0),
                in_=cTOKID[:, i:i + 1], in_offset=None,
                bounds_check=rB32K, oob_is_err=False), deps=[v_cu, z_loc, ld_consts[3]])
        prev_tile = {"of": trs[-1][0], "x1p": v_x1, "womm": mm_wo1, "log": v_log,
                     "cnt_ev": v_cnt, "sq_done": s_sq2, "scat": sc_last,
                     "otwar": v_of, "x1war": trs2[-1][0], "wwar": wr_w,
                     "s8war": sc_last, "elwar": v_sx, "x1wr": prev_x1wr, "x1fwr": wr_x1f}

    a3_final = [prev_tile[k] for k in prev_tile if prev_tile[k] is not None] + [tp_prev]
    if stage == "x1":
        DMA("sync", lambda e: e.dma_start(out=outs["DQP"][:], in_=QPH[0]), deps=[last_wr["QP"]])
        DMA("sync", lambda e: e.dma_start(out=outs["DEB"][:], in_=EBH[0]), deps=[last_wr["EB"]])
        DMA("sync", lambda e: e.dma_start(out=outs["DET"][:], in_=ETOKH[0, 0:GT, :]), deps=[last_wr["ET"]])
        DMA("sync", lambda e: e.dma_start(out=outs["DVT"][:], in_=VTOKH[0, 0:GT, :]), deps=[last_wr["VT"]])
        DMA("sync", lambda e: e.dma_start(out=outs["DOT"][:], in_=OTH[0:128, :]), deps=ot_wr[-1:])
        DMA("sync", lambda e: e.dma_start(out=outs["DGS"][:], in_=GSH[0:128, :]), deps=[last_wr["GS"]])
        return nc, P

    S("gpsimd", lambda e: None, inc=0, deps=[])
    DMAq = DMA
    wr_cnts = DMA("sync", lambda e: e.dma_start(
        out=LOCTBL[CAP * E:CAP * E + E, 0:1].rearrange("(a b) c -> a (b c)", a=1),
        in_=TPROW[:, :]), deps=[tp_prev])
    cc_loc = C("gpsimd", lambda e: e.collective_compute(
        "AllGather", ALU.bypass, replica_groups=[list(range(8))],
        ins=[LOCTBL.opt()], outs=[LOCAG.opt()]),
        deps=[wr_cnts, prev_tile["scat"], prev_tile["wwar"]])
    cc_x1 = C("gpsimd", lambda e: e.collective_compute(
        "AllGather", ALU.bypass, replica_groups=[list(range(8))],
        ins=[X1TOK.opt()], outs=[X1AG.opt()]),
        deps=[prev_tile["x1wr"], cc_loc])
    cc_w = cc_loc

    def dummy_out(dep):
        last = None
        for i2 in range(NT):
            last = DMA("sync", lambda e, i2=i2: e.dma_start(
                out=outs["OUT"][i2 * 128:(i2 + 1) * 128, :], in_=XTOK[i2 * 128:(i2 + 1) * 128, :]),
                deps=[dep] if i2 == 0 else [])
        S("sync", lambda e: None, inc=0, deps=[last])
    if stage == "moe1":
        dummy_out(cc_w)
        return nc, P

    REG.reset(A1_BASE)
    sCAG = sbR("sCAG", [8, E], F32)
    sCAGT = sbR("sCAGT", [E, 8], F32)
    sMyBT = sbR("sMyBT", [8, 8], F32)
    sPfx = sbR("sPfx", [8, 8], F32)
    sPexp = sbR("sPexp", [32, 8], F32)
    sCexp = sbR("sCexp", [32, 8], F32)
    sAccC = sbR("sAccC", [32, 128], F32)
    sAccB = sbR("sAccB", [32, 128], F32)
    sTMP = sbR("sTMP", [32, 128], F32)
    sFIDX = sbR("sFIDX", [128, 32], F32)
    sFIDXU = sbR("sFIDXU", [128, 32], U32)
    sGL = sbR("sGL", [128, 32], F32)
    sXIDXU = sbR("sXIDXU", [128, 32], U32)
    sWIDXU = sbR("sWIDXU", [128, 32], U32)
    sWCOL = sbR("sWCOL", [128, 32], F32)
    wgb = [sbR(f"wgb{k}", [128, 8 * FF], BF16) for k in range(2)]
    wub = [sbR(f"wub{k}", [128, 8 * FF], BF16) for k in range(2)]
    wdb = [sbR(f"wdb{k}", [128, 6 * 1024], BF16) for k in range(2)]
    XBT = sbR("XBT", [128, 4 * 1024], BF16)
    XBF = sbR("XBF", [128, 8 * 512], BF16)
    GSb = sbR("GSb", [128, 6 * 512], BF16)
    USb = sbR("USb", [128, 6 * 512], BF16)
    HSb = sbR("HSb", [128, 6 * 512], BF16)
    YST = [sbR(f"YST{k}", [128, 1024], BF16) for k in range(2)]
    SQdbg = sbR("SQdbg", [128, 512], F32)

    ld_cag = DMA("sync", lambda e: e.dma_start(
        out=sCAG[:], in_=LOCAG[:, 0:1].rearrange("(c r) one -> c (r one)", r=CAP * E + 128 + T * E)[:, CAP * E:CAP * E + E]),
        deps=[cc_loc] + a3_final)
    mm_br = C("tensor", lambda e: e.matmul(p[2][0:1, 0:E], cPREM[:], sCAG[:], start=True, stop=True),
              deps=[ld_cag, ld_consts[4]])
    v_br = C("vector", lambda e: e.tensor_copy(out=BASER[:], in_=p[2][0:1, 0:E]), deps=[mm_br])
    mm_bb = C("tensor", lambda e: e.matmul(p[3][:, 0:E], cONEROW[:], BASER[:], start=True, stop=True),
              deps=[v_br, c_ones])
    v_bb = C("vector", lambda e: e.tensor_copy(out=BASEBC[:], in_=p[3][:, 0:E]), deps=[mm_bb])
    mm_ct = C("tensor", lambda e: e.transpose(p[4][0:E, 0:8], sCAG[:], cIDF[0:8, 0:8]),
              deps=[ld_cag, ld_consts[1], mm_bb])
    v_ct = C("vector", lambda e: e.tensor_copy(out=sCAGT[:], in_=p[4][0:E, 0:8]), deps=[mm_ct])
    mm_mb = C("tensor", lambda e: e.matmul(p[2][0:8, 0:8], cSEL[:], sCAGT[:], start=True, stop=True),
              deps=[v_ct, ld_consts[5], v_br])
    v_mb = C("vector", lambda e: e.tensor_copy(out=sMyBT[:], in_=p[2][0:8, 0:8]), deps=[mm_mb])
    v_px = C("vector", lambda e: e.tensor_tensor_scan(sPfx[:], sMyBT[:], cZ[0:8, 0:8],
                                                      0.0, ALU.add, ALU.add), deps=[v_mb])
    v_px2 = C("vector", lambda e: e.tensor_sub(sPfx[:], sPfx[:], sMyBT[:]), deps=[v_px])
    mm_pe = C("tensor", lambda e: e.matmul(p[3][0:32, 0:8], cEXP48[:], sPfx[:], start=True, stop=True),
              deps=[v_px2, ld_consts[6], v_bb])
    v_pe = C("vector", lambda e: e.tensor_copy(out=sPexp[:], in_=p[3][0:32, 0:8]), deps=[mm_pe])
    mm_ce = C("tensor", lambda e: e.matmul(p[2][0:32, 0:8], cEXP48[:], sMyBT[:], start=True, stop=True),
              deps=[v_mb, mm_pe])
    v_ce = C("vector", lambda e: e.tensor_copy(out=sCexp[:], in_=p[2][0:32, 0:8]), deps=[mm_ce])
    if stage == "moe2a":
        dummy_out(v_ce)
        return nc, P
    m_ac = C("vector", lambda e: e.memset(sAccC[:], 0.0), deps=a3_final)
    m_ab = C("vector", lambda e: e.memset(sAccB[:], 0.0), deps=a3_final)
    accc, accb = m_ac, m_ab
    for cc_ in range(1, 8):
        ge = C("vector", lambda e, cc_=cc_: e.tensor_single_scalar(
            sTMP[:], cPVAL[:], sPexp[:, cc_:cc_ + 1], ALU.is_ge), deps=[v_pe, accb, ld_consts[7]])
        a1_ = C("vector", lambda e: e.tensor_add(sAccC[:], sAccC[:], sTMP[:]), deps=[ge, accc])
        a2_ = C("vector", lambda e, cc_=cc_: e.scalar_tensor_tensor(
            sAccB[:], sTMP[:], sCexp[:, cc_ - 1:cc_], sAccB[:], ALU.mult, ALU.add),
            deps=[ge, v_ce, a1_])
        accc, accb = a1_, a2_
    f1 = C("vector", lambda e: e.tensor_scalar(sTMP[:], sAccC[:], float(CAP * E + 128 + T * E), 0.0,
                                               ALU.mult, ALU.add), deps=[accc, accb])
    f2 = C("vector", lambda e: e.tensor_add(sTMP[:], sTMP[:], cPVAL[:]), deps=[f1])
    f3 = C("vector", lambda e: e.tensor_sub(sTMP[:], sTMP[:], sAccB[:]), deps=[f2])
    f4 = C("vector", lambda e: e.tensor_scalar_add(sTMP[:], sTMP[:], cEOFF[:, 0:1]), deps=[f3, ld_consts[8]])
    mm_ft = C("tensor", lambda e: e.transpose(p[5][:, 0:32].bitcast(F32), sTMP[:], cIDF[0:32, 0:32]),
              deps=[f4, mm_ce])
    v_ft = C("vector", lambda e: e.tensor_copy(out=sFIDX[:], in_=p[5][:, 0:32].bitcast(F32)), deps=[mm_ft])
    v_fu = C("vector", lambda e: e.tensor_copy(out=sFIDXU[:], in_=sFIDX[:]), deps=[v_ft])
    if stage == "moe2b":
        dummy_out(v_fu)
        return nc, P
    gl_done = []
    for r_ in range(32):
        gstep = DMA("gpsimd", lambda e, r_=r_: e.indirect_dma_start(
            out=sGL[:, r_:r_ + 1], out_offset=None,
            in_=LOCAG[:, :], in_offset=IndirectOffsetOnAxis(sFIDXU[:, r_:r_ + 1], 0),
            bounds_check=rB256K, oob_is_err=False), deps=[v_fu, cc_loc])
        gl_done.append(gstep)
    if stage == "moe2c":
        dummy_out(gl_done[-1])
        return nc, P
    sOO = sbR("sOO", [128, 32], F32)
    OOB32 = sbR("OOB32", [128, 32], F32)
    v_oo = C("vector", lambda e: e.tensor_single_scalar(sOO[:], sGL[:], 0.5, ALU.is_lt),
             deps=gl_done[-1:])
    v_ob = C("vector", lambda e: e.tensor_scalar_mul(OOB32[:], sOO[:], 2.0e9),
             deps=[v_oo])
    v_x1i = C("vector", lambda e: e.tensor_scalar_add(sGL[:], sGL[:], -1.0), deps=[v_ob] + gl_done)
    v_x2i = C("vector", lambda e: e.tensor_add(sFIDX[:], sGL[:], OOB32[:]), deps=[v_x1i])
    v_x3i = C("vector", lambda e: e.tensor_copy(out=sXIDXU[:], in_=sFIDX[:]), deps=[v_x2i])
    v_w1 = C("vector", lambda e: e.tensor_scalar(sFIDX[:], sGL[:], 64.0, 0.0, ALU.mult, ALU.add),
             deps=[v_x3i])
    v_w2 = C("vector", lambda e: e.tensor_add(sFIDX[:], sFIDX[:], cECON[:]), deps=[v_w1, ld_consts[9]])
    sCC = sbR("sCC", [128, 32], F32)
    v_wc0 = C("vector", lambda e: e.memset(sCC[:], 0.0), deps=[v_ct])
    _pw = v_wc0
    for cc2 in range(1, 8):
        _pw = C("vector", lambda e, cc2=cc2: e.tensor_single_scalar(
            OOB32[:], sGL[:], cc2 * 4096.0 - 0.5, ALU.is_ge), deps=[v_x1i, _pw])
        _pw = C("vector", lambda e: e.tensor_add(sCC[:], sCC[:], OOB32[:]), deps=[_pw])
    v_w2b = C("vector", lambda e: e.scalar_tensor_tensor(
        sFIDX[:], sCC[:], float(CAP * E + 128), sFIDX[:], ALU.mult, ALU.add),
        deps=[v_w2, _pw])
    v_w2c = C("vector", lambda e: e.tensor_scalar_add(sFIDX[:], sFIDX[:], float(CAP * E + 128)),
              deps=[v_w2b])
    v_ob2 = C("vector", lambda e: e.tensor_scalar_mul(OOB32[:], sOO[:], 2.0e9), deps=[v_w2c])
    v_w3 = C("vector", lambda e: e.tensor_add(sFIDX[:], sFIDX[:], OOB32[:]), deps=[v_ob2])
    v_w4 = C("vector", lambda e: e.tensor_copy(out=sWIDXU[:], in_=sFIDX[:]), deps=[v_w3])
    if stage == "moe2d":
        dummy_out(v_w4)
        return nc, P
    m_wc = C("vector", lambda e: e.memset(sWCOL[:], 0.0), deps=[v_ct])
    w_done = []
    for r_ in range(32):
        wstep = DMA("gpsimd", lambda e, r_=r_: e.indirect_dma_start(
            out=sWCOL[:, r_:r_ + 1], out_offset=None,
            in_=LOCAG[:, :],
            in_offset=IndirectOffsetOnAxis(sWIDXU[:, r_:r_ + 1], 0),
            bounds_check=rB2M, oob_is_err=False), deps=[v_w4, m_wc, cc_w])
        w_done.append(wstep)

    if stage == "moe2":
        dummy_out(w_done[-1])
        return nc, P
    yb_wr = []
    prev_exp = {}
    for j in range(EPC):
        par = j % 2
        dep_w = prev_exp.get(("wload", par))
        lw1 = DMA("gpsimd", lambda e, j=j, par=par: e.dma_start(
            out=wgb[par][:].rearrange("p (c n) -> p c n", n=FF),
            in_=WGE[j].rearrange("(c p) n -> p c n", p=128)), deps=[dep_w] + (a3_final if j < 2 else []))
        lw2 = DMA("gpsimd", lambda e, j=j, par=par: e.dma_start(
            out=wub[par][:].rearrange("p (c n) -> p c n", n=FF),
            in_=WUE[j].rearrange("(c p) n -> p c n", p=128)), deps=[lw1])
        lw3 = DMA("gpsimd", lambda e, j=j, par=par: e.dma_start(
            out=wdb[par][:].rearrange("p (c n) -> p c n", n=1024),
            in_=WDE[j].rearrange("(c p) n -> p c n", p=128)), deps=[lw2])
        mz = C("vector", lambda e: e.memset(XBT[:].bitcast(F32), 0.0), deps=[prev_exp.get("xbf_done"), v_x3i])
        gx = []
        for pc in range(4):
            gx.append(DMA("gpsimd", lambda e, j=j, pc=pc: e.indirect_dma_start(
                out=XBT[:, pc * 1024:(pc + 1) * 1024],
                out_offset=None,
                in_=X1AG[:, :], in_offset=IndirectOffsetOnAxis(sXIDXU[:, 4 * j + pc:4 * j + pc + 1], 0),
                bounds_check=rB32K, oob_is_err=False), deps=[mz, cc_x1, lw3]))
        trx = []
        for pc in range(4):
            for dc in range(8):
                pt = pt6 if (pc * 8 + dc) % 2 == 0 else pt7
                tr = C("tensor", lambda e, pc=pc, dc=dc, pt=pt: e.transpose(
                    pt[:, 0:128], XBT[:, pc * 1024 + dc * 128: pc * 1024 + (dc + 1) * 128], cID[:]),
                    deps=[gx[pc], trx[-2][1] if len(trx) >= 2 else prev_exp.get("gu_done"), ld_cid])
                ev = C("vector", lambda e, pc=pc, dc=dc, pt=pt: e.tensor_copy(
                    out=XBF[:, dc * 512 + pc * 128: dc * 512 + (pc + 1) * 128], in_=pt[:, 0:128]),
                    deps=[tr])
                trx.append((tr, ev))
        xbf_done = [ev for _, ev in trx]
        h_done = []
        for fc in range(6):
            def g_mm(e, fc=fc, par=par):
                for dc in range(8):
                    ins = e.matmul(p[0][:, 0:512], wgb[par][:, dc * FF + fc * 128: dc * FF + (fc + 1) * 128],
                                   XBF[:, dc * 512:(dc + 1) * 512], start=(dc == 0), stop=(dc == 7))
                return ins
            mmg = C("tensor", g_mm, deps=xbf_done + [lw1, prev_exp.get(("evg", fc))])
            evg = C("scalar", lambda e, fc=fc: e.activation(out=GSb[:, fc * 512:(fc + 1) * 512],
                                                            in_=p[0][:, 0:512], func=AF.Silu), deps=[mmg])
            def u_mm(e, fc=fc, par=par):
                for dc in range(8):
                    ins = e.matmul(p[1][:, 0:512], wub[par][:, dc * FF + fc * 128: dc * FF + (fc + 1) * 128],
                                   XBF[:, dc * 512:(dc + 1) * 512], start=(dc == 0), stop=(dc == 7))
                return ins
            mmu = C("tensor", u_mm, deps=xbf_done + [lw2, prev_exp.get(("evu", fc)), evg])
            evu = C("vector", lambda e, fc=fc: e.tensor_copy(out=USb[:, fc * 512:(fc + 1) * 512],
                                                             in_=p[1][:, 0:512]), deps=[mmu])
            hh = C("vector", lambda e, fc=fc: e.tensor_tensor(
                HSb[:, fc * 512:(fc + 1) * 512], GSb[:, fc * 512:(fc + 1) * 512],
                USb[:, fc * 512:(fc + 1) * 512], ALU.mult), deps=[evu, evg, prev_exp.get(("hh", fc))])
            prev_exp[("evg", fc)] = evg
            prev_exp[("evu", fc)] = evu
            prev_exp[("hh", fc)] = hh
            h_done.append(hh)
        ywr = []
        for tg in range(4):
            def y_mm(e, tg=tg, par=par, half=0):
                for fc in range(6):
                    ins = e.matmul(p[2][:, 0:512], HSb[:, fc * 512 + tg * 128: fc * 512 + (tg + 1) * 128],
                                   wdb[par][:, fc * 1024: fc * 1024 + 512], start=(fc == 0), stop=(fc == 5))
                return ins
            def y_mm2(e, tg=tg, par=par):
                for fc in range(6):
                    ins = e.matmul(p[3][:, 0:512], HSb[:, fc * 512 + tg * 128: fc * 512 + (tg + 1) * 128],
                                   wdb[par][:, fc * 1024 + 512: (fc + 1) * 1024], start=(fc == 0), stop=(fc == 5))
                return ins
            mmy1 = C("tensor", y_mm, deps=h_done + [lw3, prev_exp.get(("ye", tg, 0))])
            if stage == "full_dbg" and j == 0 and tg == 0:
                dyp = C("vector", lambda e: e.tensor_copy(out=SQdbg[:], in_=p[2][:, 0:512]), deps=[mmy1])
                DMA("sync", lambda e: e.dma_start(out=outs["DYP"][:], in_=SQdbg[:]), deps=[dyp])
            mmy2 = C("tensor", y_mm2, deps=h_done + [lw3, prev_exp.get(("ye", tg, 1))])
            yp = (4 * j + tg) % 2
            ev1 = C("vector", lambda e, j=j, tg=tg, yp=yp: e.tensor_scalar_mul(
                YST[yp][:, 0:512], p[2][:, 0:512], sWCOL[:, 4 * j + tg:4 * j + tg + 1]),
                deps=[mmy1] + w_done[-1:] + [prev_exp.get(("ystwr", yp))])
            ev2 = C("vector", lambda e, j=j, tg=tg, yp=yp: e.tensor_scalar_mul(
                YST[yp][:, 512:1024], p[3][:, 0:512], sWCOL[:, 4 * j + tg:4 * j + tg + 1]),
                deps=[mmy2] + w_done[-1:])
            wr = DMA("sync", lambda e, j=j, tg=tg, yp=yp: e.dma_start(
                out=YB[j * CAP + tg * 128: j * CAP + (tg + 1) * 128, :], in_=YST[yp][:]),
                deps=[ev1, ev2])
            prev_exp[("ystwr", yp)] = wr

            prev_exp[("ye", tg, 0)] = ev1
            prev_exp[("ye", tg, 1)] = ev2
            prev_exp[("yst", tg)] = wr
            ywr.append(wr)
        if stage == "full_dbg" and j == 0:
            dx1 = DMA("sync", lambda e: e.dma_start(out=outs["DXB"][:], in_=XBT[:]), deps=gx)
            dx2 = DMA("sync", lambda e: e.dma_start(out=outs["DXBF"][:], in_=XBF[:]), deps=xbf_done)
            dx3 = DMA("sync", lambda e: e.dma_start(out=outs["DGSB"][:], in_=GSb[:]), deps=h_done)
            dx4 = DMA("sync", lambda e: e.dma_start(out=outs["DHSB"][:], in_=HSb[:]), deps=h_done)
            ywr.append(dx1); ywr.append(dx2); ywr.append(dx3); ywr.append(dx4)
        prev_exp[("wload", par)] = ywr[-1]
        prev_exp["xbf_done"] = xbf_done[-1]
        prev_exp["gu_done"] = h_done[-1]
        yb_wr += ywr
    dmp1 = DMA("sync", lambda e: e.dma_start(out=outs["DGL"][:], in_=sGL[:]), deps=[v_x1i]) if stage == "full_dbg" else None
    dmp2 = DMA("sync", lambda e: e.dma_start(out=outs["DWC"][:], in_=sWCOL[:]), deps=w_done) if stage == "full_dbg" else None
    dmp3 = DMA("sync", lambda e: e.dma_start(out=outs["DXI"][:], in_=sXIDXU[:]), deps=[v_x3i]) if stage == "full_dbg" else None
    b_final = yb_wr[-4:] + [prev_exp[("hh", fc)] for fc in range(6)] + [d_ for d_ in (dmp1, dmp2, dmp3) if d_]

    cc_y = C("gpsimd", lambda e: e.collective_compute(
        "AllGather", ALU.bypass, replica_groups=[list(range(8))],
        ins=[YB.opt()], outs=[YAG.opt()]), deps=yb_wr)

    if stage == "moe3":
        dummy_out(cc_y)
        return nc, P
    REG.reset(A1_BASE)
    wsg = sbR("wsg", [128, 8 * 1536], BF16)
    wsd = sbR("wsd", [128, 6 * 1024], BF16)
    X1F2 = sbR("X1F2", [128, 4096], BF16)
    SGb = sbR("SGb", [128, 6 * 512], BF16)
    SUb = sbR("SUb", [128, 6 * 512], BF16)
    HSh = sbR("HSh", [128, 6 * 512], BF16)
    Yg = [sbR(f"Yg{k}", [128, 8 * 1024], BF16) for k in range(2)]
    GPb = [sbR(f"GPb{k}", [128, E], F32) for k in range(2)]
    VVb = [sbR(f"VVb{k}", [128, E], F32) for k in range(2)]
    T1b = [sbR(f"T1b{k}", [128, E], F32) for k in range(2)]
    S8c = [sbR(f"S8c{k}", [128, 8], F32) for k in range(2)]
    S8cu = [sbR(f"S8cu{k}", [128, 8], U32) for k in range(2)]
    X1B2 = [sbR(f"X1B2{k}", [128, 1024], BF16) for k in range(2)]
    C1b = [sbR(f"C1b{k}", [128, 1024], F32) for k in range(2)]
    SQ2 = [sbR(f"SQ2{k}", [128, 1024], F32) for k in range(2)]
    CL2 = [sbR(f"CL2{k}", [128, 8], F32) for k in range(2)]
    OUTt = [sbR(f"OUTt{k}", [128, 1024], F32) for k in range(2)]

    lws1 = DMA("gpsimd", lambda e: e.dma_start(
        out=wsg[:].rearrange("p (c n) -> p c n", n=1536),
        in_=WSGU[:].rearrange("(c p) n -> p c n", p=128)), deps=b_final + [cc_y])
    lws2 = DMA("gpsimd", lambda e: e.dma_start(
        out=wsd[:].rearrange("p (c n) -> p c n", n=1024),
        in_=WSD[:].rearrange("(c p) n -> p c n", p=128)), deps=[lws1])
    out_wr = []
    prevc = {}
    for g in range(NG):
        t0g = g * GT
        ldx1f = DMA("sync", lambda e, t0g=t0g: e.dma_start(
            out=X1F2[:].rearrange("p (c t) -> p c t", t=GT),
            in_=X1FMH[:, :, t0g:t0g + GT].rearrange("c p t -> p c t")),
            deps=[prev_tile["x1fwr"], prevc.get("gu_last")] + (b_final if g == 0 else []))
        hs_done = []
        for fc in range(12):
            def gu_mm(e, fc=fc):
                for dc in range(8):
                    ins = e.matmul(p[0][:, 0:512], wsg[:, dc * 1536 + fc * 128: dc * 1536 + (fc + 1) * 128],
                                   X1F2[:, dc * GT:(dc + 1) * GT], start=(dc == 0), stop=(dc == 7))
                return ins
            mmgu = C("tensor", gu_mm, deps=[ldx1f, lws1, prevc.get(("sev", fc)),
                                            hs_done[fc - 1] if fc > 0 else prevc.get("odd_rout"),
                                            prevc.get("gu_last")])
            if fc < 6:
                sev = C("scalar", lambda e, fc=fc: e.activation(
                    out=SGb[:, fc * 512:(fc + 1) * 512], in_=p[0][:, 0:512], func=AF.Silu), deps=[mmgu])
            else:
                sev = C("vector", lambda e, fc=fc: e.tensor_copy(
                    out=SUb[:, (fc - 6) * 512:(fc - 5) * 512], in_=p[0][:, 0:512]), deps=[mmgu])
            prevc[("sev", fc)] = sev
            hs_done.append(sev)
        hmul = []
        for fc in range(6):
            hh = C("vector", lambda e, fc=fc: e.tensor_tensor(
                HSh[:, fc * 512:(fc + 1) * 512], SGb[:, fc * 512:(fc + 1) * 512],
                SUb[:, fc * 512:(fc + 1) * 512], ALU.mult),
                deps=[hs_done[fc], hs_done[6 + fc], prevc.get(("hsh", fc))])
            prevc[("hsh", fc)] = hh
            hmul.append(hh)
        prevc["gu_last"] = hmul[-1]
        for i in range(4):
            gi = 4 * g + i
            t0 = gi * 128
            par = gi % 2
            dp2 = prevc
            pr1 = p[4] if par == 0 else p[2]
            pr2 = p[5] if par == 0 else p[3]
            vgp = C("vector", lambda e, gi=gi, par=par: e.tensor_tensor(
                GPb[par][:], CNT[:, gi * E:(gi + 1) * E], BASEBC[:], ALU.add),
                deps=[v_bb, dp2.get(("vgp_war", par))] + (b_final + [cc_y] if gi == 0 else []))
            vv1 = C("vector", lambda e, par=par: e.tensor_single_scalar(
                VVb[par][:], GPb[par][:], 511.5, ALU.is_lt), deps=[vgp])
            vv2 = C("vector", lambda e, gi=gi, par=par: e.tensor_tensor(
                VVb[par][:], VVb[par][:], MSK[:, gi * E:(gi + 1) * E], ALU.mult), deps=[vv1])
            vt1 = C("vector", lambda e, par=par: e.tensor_tensor(
                T1b[par][:], GPb[par][:], cIOTAE[:], ALU.add), deps=[vgp, vv2])
            vt2 = C("vector", lambda e, par=par: e.tensor_tensor(
                T1b[par][:], T1b[par][:], VVb[par][:], ALU.mult), deps=[vt1])
            vt3 = C("vector", lambda e, par=par: e.tensor_scalar_add(
                T1b[par][:], T1b[par][:], -1.0), deps=[vt2])
            vm8 = C("vector", lambda e, par=par: e.max(S8c[par][:], T1b[par][:]), deps=[vt3])
            vo1 = C("vector", lambda e, par=par: e.tensor_single_scalar(
                GPb[par][:, 0:8], S8c[par][:], 0.0, ALU.is_lt), deps=[vm8])
            vo2 = C("vector", lambda e, par=par: e.scalar_tensor_tensor(
                S8c[par][:], GPb[par][:, 0:8], 2.0e9, S8c[par][:], ALU.mult, ALU.add), deps=[vo1])
            vo3 = C("vector", lambda e, par=par: e.tensor_copy(out=S8cu[par][:], in_=S8c[par][:]),
                    deps=[vo2, dp2.get(("s8war", par))])
            mzy = C("vector", lambda e, par=par: e.memset(Yg[par][:].bitcast(F32), 0.0),
                    deps=[dp2.get(("ygwar", par))] + (b_final if gi < 2 else []))
            yg_g = []
            for jj in range(8):
                yg_g.append(DMA("gpsimd", lambda e, jj=jj, par=par: e.indirect_dma_start(
                    out=Yg[par][:, jj * 1024:(jj + 1) * 1024],
                    out_offset=None,
                    in_=YAG[:, :], in_offset=IndirectOffsetOnAxis(S8cu[par][:, jj:jj + 1], 0),
                    bounds_check=rB32K, oob_is_err=False), deps=[mzy, vo3, cc_y, lws2]))
            dp2[("s8war", par)] = yg_g[-1]
            def rd_mm(e, par=par, pr1=pr1, i=i):
                for fc in range(6):
                    e.matmul(pr1[:, 0:512], HSh[:, fc * 512 + i * 128: fc * 512 + (i + 1) * 128],
                             wsd[:, fc * 1024: fc * 1024 + 512], start=(fc == 0), stop=False)
                for jj in range(8):
                    ins = e.matmul(pr1[:, 0:512], cID[:], Yg[par][:, jj * 1024: jj * 1024 + 512],
                                   start=False, stop=(jj == 7))
                return ins
            def rd_mm2(e, par=par, pr2=pr2, i=i):
                for fc in range(6):
                    e.matmul(pr2[:, 0:512], HSh[:, fc * 512 + i * 128: fc * 512 + (i + 1) * 128],
                             wsd[:, fc * 1024 + 512: (fc + 1) * 1024], start=(fc == 0), stop=False)
                for jj in range(8):
                    ins = e.matmul(pr2[:, 0:512], cID[:], Yg[par][:, jj * 1024 + 512: (jj + 1) * 1024],
                                   start=False, stop=(jj == 7))
                return ins
            rdeps = yg_g + [ld_cid, lws2, dp2.get(("c1war", par))] + hmul
            if par == 1:
                rdeps = rdeps + hs_done
            mr1 = C("tensor", rd_mm, deps=rdeps)
            mr2 = C("tensor", rd_mm2, deps=rdeps + [mr1])
            dp2[("ygwar", par)] = mr2
            ldx1b = DMA("sync", lambda e, t0=t0, par=par: e.dma_start(
                out=X1B2[par][:], in_=X1TOK[t0:t0 + 128, :]),
                deps=[prev_tile["x1wr"], dp2.get(("x1b2war", par))] + (b_final if gi == 0 else []))
            c2a = C("vector", lambda e, par=par, pr1=pr1: e.tensor_tensor(
                C1b[par][:, 0:512], X1B2[par][:, 0:512], pr1[:, 0:512], ALU.add), deps=[mr1, ldx1b])
            c2 = C("vector", lambda e, par=par, pr2=pr2: e.tensor_tensor(
                C1b[par][:, 512:1024], X1B2[par][:, 512:1024], pr2[:, 0:512], ALU.add),
                deps=[mr2, ldx1b, c2a])
            dp2[("x1b2war", par)] = c2
            dp2[("c1war", par)] = c2
            if par == 1:
                dp2["odd_rout"] = c2
            vnm = C("vector", lambda e, par=par: e.tensor_reduce(
                CL2[par][:, 0:1], C1b[par][:], AX.X, ALU.add, negate=True), deps=[c2])
            vnm2 = C("vector", lambda e, par=par: e.tensor_scalar_mul(
                CL2[par][:, 0:1], CL2[par][:, 0:1], 1.0 / D), deps=[vnm])
            ssq = C("scalar", lambda e, par=par: e.activation(
                out=SQ2[par][:], in_=C1b[par][:], func=AF.Square), deps=[c2, dp2.get(("sq2war", par))])
            vsq = C("vector", lambda e, par=par: e.reduce_sum(
                CL2[par][:, 1:2], SQ2[par][:], axis=AX.X), deps=[ssq])
            dp2[("sq2war", par)] = vsq
            vm2 = C("vector", lambda e, par=par: e.tensor_tensor(
                CL2[par][:, 2:3], CL2[par][:, 0:1], CL2[par][:, 0:1], ALU.mult), deps=[vnm2])
            vvar = C("vector", lambda e, par=par: e.tensor_scalar(
                CL2[par][:, 1:2], CL2[par][:, 1:2], 1.0 / D, 1e-5, ALU.mult, ALU.add), deps=[vsq])
            vvar2 = C("vector", lambda e, par=par: e.tensor_sub(
                CL2[par][:, 1:2], CL2[par][:, 1:2], CL2[par][:, 2:3]), deps=[vvar, vm2])
            ssd = C("scalar", lambda e, par=par: e.activation(
                out=CL2[par][:, 2:3], in_=CL2[par][:, 1:2], func=AF.Sqrt), deps=[vvar2])
            vrsd = C("vector", lambda e, par=par: e.reciprocal(
                CL2[par][:, 3:4], CL2[par][:, 2:3]), deps=[ssd])
            vout = C("vector", lambda e, par=par: e.scalar_tensor_tensor(
                OUTt[par][:], C1b[par][:], CL2[par][:, 0:1],
                CL2[par][:, 3:4].broadcast_to([128, 1024]), ALU.add, ALU.mult),
                deps=[vrsd, dp2.get(("outwar", par))])
            wout = DMA("sync", lambda e, t0=t0, par=par: e.dma_start(
                out=outs["OUT"][t0:t0 + 128, :], in_=OUTt[par][:]), deps=[vout])
            dp2[("outwar", par)] = wout
            dp2[("vgp_war", par)] = vt3
            out_wr.append(wout)
    S("sync", lambda e: None, inc=0, deps=out_wr[-1:] + out_wr[:1])
    return nc, P


_NC_CACHE = {}

def _build_cached():
    if "nc" not in _NC_CACHE:
        nc, P = build(stage="full")
        with nc.Block() as block:
            P.build(block)
        _NC_CACHE["nc"] = nc
    return _NC_CACHE["nc"]


def kernel(L0_attn_hidden_states, L0_attn_Wq, L0_attn_Wk, L0_attn_Wv,
           L0_attn_Wg1, L0_attn_Wg2, L0_attn_g_norm_weight, L0_attn_Wo,
           L0_ffn_router_w, L0_ffn_w_gate, L0_ffn_w_up, L0_ffn_w_down,
           L0_ffn_ws_gate_up, L0_ffn_ws_down):
    x = np.ascontiguousarray(np.asarray(L0_attn_hidden_states, dtype=np.float32))
    Wq = np.asarray(L0_attn_Wq, np.float32); Wk = np.asarray(L0_attn_Wk, np.float32)
    Wv = np.asarray(L0_attn_Wv, np.float32); Wo = np.asarray(L0_attn_Wo, np.float32)
    Wg1 = np.asarray(L0_attn_Wg1, np.float32); Wg2 = np.asarray(L0_attn_Wg2, np.float32)
    gnw = np.asarray(L0_attn_g_norm_weight, np.float32)
    rw = np.asarray(L0_ffn_router_w, np.float32)
    wge = np.asarray(L0_ffn_w_gate, np.float32); wue = np.asarray(L0_ffn_w_up, np.float32)
    wde = np.asarray(L0_ffn_w_down, np.float32)
    wsgu = np.ascontiguousarray(np.asarray(L0_ffn_ws_gate_up, np.float32))
    wsd = np.ascontiguousarray(np.asarray(L0_ffn_ws_down, np.float32))

    WQT = np.ascontiguousarray(Wq.T); WKT = np.ascontiguousarray(Wk.T)
    WVT = np.ascontiguousarray(Wv.T); WOT = np.ascontiguousarray(Wo.T)
    WG1T = np.ascontiguousarray(Wg1.T); WG2T = np.ascontiguousarray(Wg2.T)
    RWT = np.ascontiguousarray(rw.T)
    GNW = np.ascontiguousarray(gnw.reshape(8, 128).T)
    r = np.arange(128)
    UTm = (r[:, None] <= r[None, :]).astype(np.float32)
    IDm = np.eye(128, dtype=np.float32)
    IOTAEm = np.broadcast_to((np.arange(64) * 512 + 1).astype(np.float32), (128, 64)).copy()
    EXP48m = (np.arange(32)[None, :] // 4 == np.arange(8)[:, None]).astype(np.float32)
    PVALm = ((np.arange(32) % 4)[:, None] * 128 + np.arange(128)[None, :]).astype(np.float32)

    in_maps = []
    for c in range(8):
        d = {
            "XT": np.ascontiguousarray(x[c].T), "XTOK": np.ascontiguousarray(x[c]),
            "WQT": WQT, "WKT": WKT, "WVT": WVT, "WOT": WOT,
            "WG1T": WG1T, "WG2T": WG2T, "GNW": GNW, "RWT": RWT,
            "WSGU": wsgu, "WSD": wsd,
            "WGE": np.ascontiguousarray(wge[8 * c:8 * c + 8]),
            "WUE": np.ascontiguousarray(wue[8 * c:8 * c + 8]),
            "WDE": np.ascontiguousarray(wde[8 * c:8 * c + 8]),
            "UT": UTm, "MST": UTm, "IDT": IDm, "IOTAE": IOTAEm,
            "TOKID": (4096 * c + 128 * np.arange(32)[None, :] + r[:, None] + 1).astype(np.float32),
            "PREM": (np.arange(8)[:, None] < c).astype(np.float32),
            "SEL64": (np.arange(64)[:, None] == (8 * c + np.arange(8))[None, :]).astype(np.float32),
            "EXP48": EXP48m, "PVAL": PVALm,
            "EOFF": ((8 * c + np.arange(32) // 4) * 512).astype(np.float32)[:, None],
            "ECON": np.broadcast_to((8 * c + np.arange(32) // 4).astype(np.float32), (128, 32)).copy(),
        }
        in_maps.append(d)

    nc = _build_cached()
    res = run_bass_kernel_spmd(nc, in_maps, core_ids=list(range(8)))
    return np.stack([np.asarray(res.results[c]["OUT"]) for c in range(8)], 0)


# version 19
# speedup vs baseline: 1.1199x; 1.1199x over previous
import numpy as np
import concourse.bass as bass
import concourse.mybir as mybir
from concourse.bass_core_rust import IndirectOffsetOnAxis
from concourse.bass_utils import run_bass_kernel_spmd

F32 = mybir.dt.float32
BF16 = mybir.dt.bfloat16
U32 = mybir.dt.uint32
AF = mybir.ActivationFunctionType
ALU = mybir.AluOpType
AX = mybir.AxisListType

ENGN = ["sync", "scalar", "vector", "tensor", "gpsimd"]


class Step:
    __slots__ = ("eng", "emit", "inc", "deps", "idx", "cum", "dma")

    def __init__(self, eng, emit, inc, deps, dma):
        self.eng = eng
        self.emit = emit
        self.inc = inc
        self.deps = deps
        self.dma = dma
        self.idx = None
        self.cum = None


class Prog:
    def __init__(self, nc):
        self.nc = nc
        self.steps = []
        self.sems = {}

    def step(self, eng, emit, *, dma=False, inc=None, deps=()):
        s = Step(eng, emit, (16 if dma else 1) if inc is None else inc,
                 [d for d in deps if d is not None], dma)
        s.idx = len(self.steps)
        self.steps.append(s)
        return s

    def dma(self, eng, emit, deps=()):
        return self.step(eng, emit, dma=True, deps=deps)

    def comp(self, eng, emit, deps=()):
        return self.step(eng, emit, dma=False, deps=deps)

    def build(self, block):
        counts = {e: 0 for e in ENGN}
        for s in self.steps:
            counts[s.eng] += s.inc
            s.cum = counts[s.eng]
        sems = self.sems

        pos = {}
        cnt = {e: 0 for e in ENGN}
        for s in self.steps:
            pos[s.idx] = cnt[s.eng]
            cnt[s.eng] += 1

        def run_engine(engobj, name):
            last_wait = {e: 0 for e in ENGN}
            for s in self.steps:
                if s.eng != name:
                    continue
                if name == "vector" and any(
                        d.eng == "vector" and not d.dma and pos[d.idx] == pos[s.idx] - 1
                        for d in s.deps):
                    engobj.drain()
                for d in s.deps:
                    need = d.cum
                    if need > last_wait[d.eng]:
                        engobj.wait_ge(sems[d.eng], need)
                        last_wait[d.eng] = need
                ins = s.emit(engobj)
                if s.inc:
                    assert ins is not None, f"step {s.idx} on {name} returned no instruction"
                    ins.then_inc(sems[name], s.inc)

        getattr(block, "sync")(lambda e: run_engine(e, "sync"))
        getattr(block, "scalar")(lambda e: run_engine(e, "scalar"))
        getattr(block, "vector")(lambda e: run_engine(e, "vector"))
        getattr(block, "tensor")(lambda e: run_engine(e, "tensor"))
        getattr(block, "gpsimd")(lambda e: run_engine(e, "gpsimd"))


"""Full megakernel builder (one NeuronCore program, SPMD across 8 cores).

stage: "x1"  -> compute through LN1 (x1) and write X1DBG output, stop.
       "full"-> full forward, write OUT.
"""

T = 4096
D = 1024
H = 8
DK = 128
NG = 8
GT = 512
NT = 32
E = 64
CAP = 512
EPC = 8
FF = 768
SF = 768


def build(stage="x1", cores=8):
    nc = bass.Bass("TRN2", target_bir_lowering=False)
    dram = lambda n, s, d, k="ExternalInput": nc.dram_tensor(n, s, d, kind=k).ap()
    idram = lambda n, s, d: nc.dram_tensor(n, s, d, kind="Internal").ap()

    XT = dram("XT", [D, T], F32)
    XTOK = dram("XTOK", [T, D], F32)
    WQT = dram("WQT", [D, D], F32)
    WKT = dram("WKT", [D, D], F32)
    WVT = dram("WVT", [D, D], F32)
    WOT = dram("WOT", [D, D], F32)
    WG1T = dram("WG1T", [D, 128], F32)
    WG2T = dram("WG2T", [128, D], F32)
    GNW = dram("GNW", [128, 8], F32)
    RWT = dram("RWT", [D, E], F32)
    WSGU = dram("WSGU", [D, 2 * SF], F32)
    WSD = dram("WSD", [SF, D], F32)
    WGE = dram("WGE", [EPC, D, FF], F32)
    WUE = dram("WUE", [EPC, D, FF], F32)
    WDE = dram("WDE", [EPC, FF, D], F32)
    UT = dram("UT", [128, 128], F32)
    MST = dram("MST", [128, 128], F32)
    IDT = dram("IDT", [128, 128], F32)
    IOTAE = dram("IOTAE", [128, E], F32)
    TOKID = dram("TOKID", [128, NT], F32)
    PREM = dram("PREM", [8, 1], F32)
    SEL64 = dram("SEL64", [E, EPC], F32)
    EXP48 = dram("EXP48", [8, 32], F32)
    PVAL = dram("PVAL", [32, 128], F32)
    EOFF = dram("EOFF", [32, 1], F32)
    ECON = dram("ECON", [128, 32], F32)

    QPH = idram("QPH", [H, 128, T], BF16)
    EBH = idram("EBH", [H, 128, T], BF16)
    ETOKH = idram("ETOKH", [H, T, 128], BF16)
    VTOKH = idram("VTOKH", [H, T, 128], BF16)
    GSH = idram("GSH", [T, D], BF16)
    OTH = idram("OTH", [T, D], BF16)
    X1TOK = idram("X1TOK", [T, D], BF16)
    X1FMH = idram("X1FMH", [8, 128, T], BF16)
    LOCTBL = idram("LOCTBL", [CAP * E + 128 + T * E, 1], F32)
    LOCAG = nc.dram_tensor("LOCAG", [8 * (CAP * E + 128 + T * E), 1], F32, kind="Internal", addr_space="Shared").ap()
    X1AG = nc.dram_tensor("X1AG", [8 * T, D], BF16, kind="Internal", addr_space="Shared").ap()
    YB = idram("YB", [EPC * CAP, D], BF16)
    YAG = nc.dram_tensor("YAG", [8 * EPC * CAP, D], BF16, kind="Internal", addr_space="Shared").ap()

    outs = {}
    if stage == "x1":
        outs["X1DBG"] = dram("X1DBG", [T, D], F32, "ExternalOutput")
        outs["LOGDBG"] = dram("LOGDBG", [T, E], F32, "ExternalOutput")
        for nm, shp in [("DQP", [128, T]), ("DEB", [128, T]), ("DET", [GT, 128]),
                        ("DVT", [GT, 128]), ("DOT", [128, D]), ("DGS", [128, D])]:
            outs[nm] = dram(nm, shp, BF16, "ExternalOutput")
        for nm in ["DAH", "DRH", "DQS"]:
            outs[nm] = dram(nm, [128, 520], F32, "ExternalOutput")
        outs["DCA"] = dram("DCA", [128, H], F32, "ExternalOutput")
        outs["DEH"] = dram("DEH", [128, 8], F32, "ExternalOutput")
        outs["DOF"] = dram("DOF", [128, D], BF16, "ExternalOutput")
        outs["DX1P"] = dram("DX1P", [128, D], F32, "ExternalOutput")
        outs["DCOL"] = dram("DCOL", [128, 8], F32, "ExternalOutput")
    else:
        outs["OUT"] = dram("OUT", [T, D], F32, "ExternalOutput")
        if stage == "full_dbg":
            outs["DLOC"] = dram("DLOC", [CAP * E + 128, 1], F32, "ExternalOutput")
            outs["DGL"] = dram("DGL", [128, 32], F32, "ExternalOutput")
            outs["DWC"] = dram("DWC", [128, 32], F32, "ExternalOutput")
            outs["DXI"] = dram("DXI", [128, 32], U32, "ExternalOutput")
            outs["DYB"] = dram("DYB", [256, D], BF16, "ExternalOutput")
            outs["DMSK"] = dram("DMSK", [128, NT * E], F32, "ExternalOutput")
            outs["DCNT"] = dram("DCNT", [128, NT * E], F32, "ExternalOutput")
            outs["DXB"] = dram("DXB", [128, 4096], BF16, "ExternalOutput")
            outs["DXBF"] = dram("DXBF", [128, 4096], BF16, "ExternalOutput")
            outs["DGSB"] = dram("DGSB", [128, 3072], BF16, "ExternalOutput")
            outs["DHSB"] = dram("DHSB", [128, 3072], BF16, "ExternalOutput")
            outs["DYP"] = dram("DYP", [128, 512], F32, "ExternalOutput")

    P = Prog(nc)
    sbA = nc.alloc_sbuf_tensor

    class Region:
        def __init__(self):
            self.start = None
            self.cur = None
        def init(self):
            self.start = nc.bytes_allocated(None) if False else None
        def alloc(self, name, shape, dtype):
            import math
            eb = mybir.dt.size(dtype)
            n = 1
            for s_ in shape[1:]:
                n *= s_
            nb = n * eb
            off = (self.cur + 31) // 32 * 32
            self.cur = off + nb
            assert self.cur <= 229300, f"SBUF overlay overflow at {name}: {self.cur}"
            return nc.alloc_sbuf_tensor_at(name, shape, dtype, offset=off)
        def mark(self):
            return self.cur
        def reset(self, m):
            self.cur = m
    REG = Region()

    cU = sbA("cU", [128, 128], F32)
    cIDF = sbA("cIDF", [128, 128], F32)
    cMST = sbA("cMST", [128, 128], BF16)
    cID = sbA("cID", [128, 128], BF16)
    cIOTAE = sbA("cIOTAE", [128, E], F32)
    cTOKID = sbA("cTOKID", [128, NT], F32)
    cPREM = sbA("cPREM", [8, 1], F32)
    cSEL = sbA("cSEL", [E, EPC], F32)
    cEXP48 = sbA("cEXP48", [8, 32], F32)
    cPVAL = sbA("cPVAL", [32, 128], F32)
    cEOFF = sbA("cEOFF", [32, 1], F32)
    cECON = sbA("cECON", [128, 32], F32)
    cONEROW = sbA("cONEROW", [1, 128], F32)
    cONECOL = sbA("cONECOL", [128, 1], F32)
    cZ = sbA("cZ", [128, 544], F32)
    cZB = sbA("cZB", [128, 1024], BF16)
    from concourse import bass_primitives as _bpp
    MSK_OFF = int(nc.bytes_allocated(_bpp.MemorySpace.SBUF))
    MSK = sbA("MSK", [128, NT * E], F32)
    CNT = sbA("CNT", [128, NT * E], F32)
    CARRY = sbA("CARRY", [128, 8 * H], F32)
    TPROW = sbA("TPROW", [1, E], F32)
    BASER = sbA("BASER", [1, E], F32)
    BASEBC = sbA("BASEBC", [128, E], F32)
    S8ALL = sbA("S8ALL", [128, 8 * NT], U32)
    HGP = [sbA(f"HGP{k}", [128, E], F32) for k in range(2)]
    HT1 = [sbA(f"HT1{k}", [128, E], F32) for k in range(2)]
    HS8 = [sbA(f"HS8{k}", [128, 8], F32) for k in range(2)]

    from concourse import bass_primitives as _bp
    REG.cur = int(nc.bytes_allocated(_bp.MemorySpace.SBUF))
    sbR = REG.alloc

    p = [nc.alloc_psum_tensor(f"p{i}", [128, 512], F32) for i in range(6)]
    pt6 = nc.alloc_psum_tensor("pt6", [128, 1024], BF16)
    pt7 = nc.alloc_psum_tensor("pt7", [128, 1024], BF16)

    sems = {}
    semctx = []
    for e in ["sync", "scalar", "vector", "tensor", "gpsimd", "cc"]:
        h = nc.alloc_semaphore(f"sem_{e}")
        sems[e] = h
    P.sems = sems

    S = P.step
    DMA = P.dma
    C = P.comp

    rB32K = nc.gpsimd.alloc_register("bnd32k")
    rB256K = nc.gpsimd.alloc_register("bnd256k")
    rB2M = nc.gpsimd.alloc_register("bnd2m")

    S("gpsimd", lambda e: e.reg_mov(rB32K, CAP * E - 1), inc=0)
    S("gpsimd", lambda e: e.reg_mov(rB256K, 8 * (CAP * E + 128 + T * E) - 1), inc=0)
    S("gpsimd", lambda e: e.reg_mov(rB2M, 8 * (CAP * E + 128 + T * E) - 1), inc=0)
    ld_consts = []
    for hbm, tile in [(UT, cU), (IDT, cIDF), (IOTAE, cIOTAE), (TOKID, cTOKID), (PREM, cPREM),
                      (SEL64, cSEL), (EXP48, cEXP48), (PVAL, cPVAL), (EOFF, cEOFF),
                      (ECON, cECON)]:
        ld_consts.append(DMA("sync", lambda e, a=hbm, b=tile: e.dma_start(out=b[:], in_=a[:])))
    ld_cmst = DMA("gpsimd", lambda e: e.dma_start(out=cMST[:], in_=MST[:]))
    ld_cid = DMA("gpsimd", lambda e: e.dma_start(out=cID[:], in_=IDT[:]))
    c_ones = C("vector", lambda e: e.memset(cONEROW[:], 1.0))
    c_ones2 = C("vector", lambda e: e.memset(cONECOL[:], 1.0))
    c_z = C("vector", lambda e: e.memset(cZ[:], 0.0))
    c_zb = C("vector", lambda e: e.memset(cZB[:], 0.0))
    c_carry = C("vector", lambda e: e.memset(CARRY[:], 0.0))
    c_tprow = C("vector", lambda e: e.memset(TPROW[:], 0.0))
    z_loc = DMA("sync", lambda e: e.dma_start(
        out=LOCTBL[0:CAP * E + 128, 0:1].rearrange("(a b) c -> a (b c)", a=128),
        in_=cZ[:, 0:(CAP * E + 128) // 128]), deps=[c_z])

    wq = sbR("wq", [128, 8 * 1024], BF16)
    wk = sbR("wk", [128, 8 * 1024], BF16)
    wv = sbR("wv", [128, 8 * 1024], BF16)
    wg1 = sbR("wg1", [128, 8 * 128], BF16)
    wg2 = sbR("wg2", [128, 1024], BF16)

    def ld_w(hbm, tile, ncol):
        return DMA("gpsimd", lambda e: e.dma_start(
            out=tile[:].rearrange("p (c n) -> p c n", n=ncol),
            in_=hbm[:].rearrange("(c p) n -> p c n", p=128)))
    ld_wq = ld_w(WQT, wq, 1024)
    ld_wk = ld_w(WKT, wk, 1024)
    ld_wv = ld_w(WVT, wv, 1024)
    ld_wg1 = ld_w(WG1T, wg1, 128)
    ld_wg2 = DMA("gpsimd", lambda e: e.dma_start(out=wg2[:], in_=WG2T[:]))

    A1_BASE = REG.cur
    xfm = sbR("xfm", [128, 8 * GT], BF16)
    EHb = sbR("EHb", [128, 520], F32)
    AHb = sbR("AHb", [128, 520], F32)
    RHb = sbR("RHb", [128, GT], F32)
    QSIL = sbR("QSIL", [128, GT], F32)
    QPg = sbR("QPg", [128, H * GT], BF16)
    EBg = sbR("EBg", [128, H * GT], BF16)
    VTOKg = sbR("VTOKg", [128, 4 * 1024], BF16)
    ETOKg = sbR("ETOKg", [128, 4 * 1024], BF16)
    G1b = sbR("G1b", [128, GT], BF16)
    GSg = sbR("GSg", [128, 4 * 1024], BF16)

    last_wr = {"QP": None, "EB": None, "VT": None, "ET": None, "GS": None}
    prev_q_mm = None
    prev_scan = None
    prev_recip = None
    prev_veb = None
    prev_sln = None
    prev_vqp = None
    last_xfm_reader = None
    prev_probe = None

    for g in range(NG):
        t0 = g * GT
        ld_x = DMA("gpsimd", lambda e, t0=t0: e.dma_start(
            out=xfm[:].rearrange("p (c t) -> p c t", t=GT),
            in_=XT[:, t0:t0 + GT].rearrange("(c p) t -> p c t", p=128)),
            deps=[last_wr["QP"], last_wr["EB"], last_xfm_reader])
        head_deps = []
        ebg_done = []
        qpg_done = []
        war_ebg = last_wr["EB_rd"] if "EB_rd" in last_wr else None
        war_qpg = last_wr["QP"]
        for h in range(H):
            def k_mm(e, h=h):
                for dc in range(8):
                    ins = e.matmul(p[0][:, 0:GT], wk[:, dc * 1024 + h * 128: dc * 1024 + (h + 1) * 128],
                                   xfm[:, dc * GT:(dc + 1) * GT], start=(dc == 0), stop=(dc == 7))
                return ins
            mm_k = C("tensor", k_mm, deps=[ld_x, ld_wk, prev_scan])
            s_exp = C("scalar", lambda e, h=h: e.activation(out=EBg[:, h * GT:(h + 1) * GT],
                                                            in_=p[0][:, 0:GT], func=AF.Exp),
                      deps=[mm_k, prev_scan, war_ebg, prev_recip])
            ebg_done.append(s_exp)
            v_scan = C("vector", lambda e, h=h: e.tensor_tensor_scan(
                AHb[:, 1:1 + GT], EBg[:, h * GT:(h + 1) * GT], cZ[:, 0:GT], 0.0, ALU.add, ALU.add),
                deps=[s_exp, c_z, prev_probe])
            prev_scan = v_scan
            v_add = C("vector", lambda e, h=h: e.tensor_tensor(
                EHb[:, 1:1 + GT], AHb[:, 1:1 + GT],
                CARRY[:, 8 * h:8 * h + 1].broadcast_to([128, GT]), ALU.add),
                deps=[v_scan, c_carry, prev_sln])
            v_sc = C("vector", lambda e, h=h: e.tensor_copy(out=CARRY[:, 8 * h:8 * h + 1], in_=EHb[:, GT:GT + 1]),
                     deps=[v_add, prev_probe])
            v_r = C("vector", lambda e: e.reciprocal(RHb[:], EHb[:, 1:1 + GT]),
                    deps=[v_sc, prev_vqp])
            prev_sln = v_r
            prev_recip = v_r
            if stage == "x1" and g == 1 and h == 0:
                dbc = DMA("sync", lambda e: e.dma_start(out=outs["DCA"][:], in_=CARRY[:, 0:8]), deps=[v_add])
                dbE = DMA("sync", lambda e: e.dma_start(out=outs["DEH"][:], in_=EHb[:, 0:8]), deps=[v_add])
                db1 = DMA("sync", lambda e: e.dma_start(out=outs["DAH"][:], in_=EHb[:]), deps=[v_add, v_r])
                prev_probe = db1
            def q_mm(e, h=h):
                for dc in range(8):
                    ins = e.matmul(p[1][:, 0:GT], wq[:, dc * 1024 + h * 128: dc * 1024 + (h + 1) * 128],
                                   xfm[:, dc * GT:(dc + 1) * GT], start=(dc == 0), stop=(dc == 7))
                return ins
            mm_q = C("tensor", q_mm, deps=[ld_x, ld_wq, prev_q_mm])
            s_sil = C("scalar", lambda e: e.activation(out=QSIL[:], in_=p[1][:, 0:GT], func=AF.Silu),
                      deps=[mm_q, prev_vqp, prev_probe])
            prev_q_mm = s_sil
            v_qp = C("vector", lambda e, h=h: e.tensor_tensor(
                QPg[:, h * GT:(h + 1) * GT], QSIL[:], RHb[:], ALU.mult), deps=[s_sil, v_r, war_qpg])
            if stage == "x1" and g == 1 and h == 0:
                db2 = DMA("sync", lambda e: e.dma_start(out=outs["DRH"][:, 0:GT], in_=RHb[:]), deps=[v_r, v_qp])
                db3 = DMA("sync", lambda e: e.dma_start(out=outs["DQS"][:, 0:GT], in_=QSIL[:]), deps=[s_sil, v_qp])
                prev_probe = db3
            prev_vqp = v_qp
            qpg_done.append(v_qp)
            head_deps.append(v_qp)
        wr_qp = DMA("sync", lambda e, t0=t0: e.dma_start(
            out=QPH[:, :, t0:t0 + GT].rearrange("h p t -> p h t"), in_=QPg[:].rearrange("p (h t) -> p h t", t=GT)),
            deps=qpg_done)
        wr_eb = DMA("sync", lambda e, t0=t0: e.dma_start(
            out=EBH[:, :, t0:t0 + GT].rearrange("h p t -> p h t"), in_=EBg[:].rearrange("p (h t) -> p h t", t=GT)),
            deps=ebg_done)
        last_wr["QP"] = wr_qp
        last_wr["EB"] = wr_eb

        vt_done = []
        et_done = []
        for i in range(4):
            def v_mm(e, i=i, half=0):
                for dc in range(8):
                    ins = e.matmul(p[2][:, 0:512],
                                   xfm[:, dc * GT + i * 128: dc * GT + i * 128 + 128],
                                   wv[:, dc * 1024: dc * 1024 + 512],
                                   start=(dc == 0), stop=(dc == 7))
                return ins
            def v_mm2(e, i=i):
                for dc in range(8):
                    ins = e.matmul(p[3][:, 0:512],
                                   xfm[:, dc * GT + i * 128: dc * GT + i * 128 + 128],
                                   wv[:, dc * 1024 + 512: (dc + 1) * 1024],
                                   start=(dc == 0), stop=(dc == 7))
                return ins
            dep_war = vt_done[-2] if len(vt_done) >= 2 else last_wr["VT"]
            mm_v1 = C("tensor", v_mm, deps=[ld_x, ld_wv, dep_war])
            mm_v2 = C("tensor", v_mm2, deps=[ld_x, ld_wv, dep_war])
            ev_v1 = C("scalar", lambda e, i=i: e.copy(
                out=VTOKg[:, i * 1024: i * 1024 + 512], in_=p[2][:, 0:512]), deps=[mm_v1])
            ev_v2 = C("scalar", lambda e, i=i: e.copy(
                out=VTOKg[:, i * 1024 + 512: (i + 1) * 1024], in_=p[3][:, 0:512]), deps=[mm_v2])
            vt_done += [ev_v1, ev_v2]
            for h in range(H):
                pt = pt6 if h % 2 == 0 else pt7
                dep_tr_war = et_done[-2] if len(et_done) >= 2 else None
                tr = C("tensor", lambda e, i=i, h=h, pt=pt: e.transpose(
                    pt[:, 0:128], EBg[:, h * GT + i * 128: h * GT + i * 128 + 128], cID[:]),
                    deps=[ebg_done[h], ld_cid, dep_tr_war])
                ev = C("scalar", lambda e, i=i, h=h, pt=pt: e.copy(
                    out=ETOKg[:, i * 1024 + h * 128: i * 1024 + (h + 1) * 128], in_=pt[:, 0:128]),
                    deps=[tr, last_wr["ET"] if (i == 0 and h == 0) else None])
                et_done.append(ev)
        for h in range(H):
            wr_vt = DMA("sync", lambda e, t0=t0, h=h: e.dma_start(
                out=VTOKH[h, t0:t0 + GT, :].rearrange("(i p) c -> p i c", p=128),
                in_=VTOKg[:].rearrange("p (i hc) -> p i hc", i=4)[:, :, h * 128:(h + 1) * 128]),
                deps=vt_done if h == 0 else ())
            wr_et = DMA("sync", lambda e, t0=t0, h=h: e.dma_start(
                out=ETOKH[h, t0:t0 + GT, :].rearrange("(i p) c -> p i c", p=128),
                in_=ETOKg[:].rearrange("p (i hc) -> p i hc", i=4)[:, :, h * 128:(h + 1) * 128]),
                deps=et_done if h == 0 else ())
        last_wr["VT"] = wr_vt
        last_wr["ET"] = wr_et
        last_wr["EB_rd"] = wr_et

        def g1_mm(e):
            for dc in range(8):
                ins = e.matmul(p[4][:, 0:GT], wg1[:, dc * 128: (dc + 1) * 128],
                               xfm[:, dc * GT:(dc + 1) * GT], start=(dc == 0), stop=(dc == 7))
            return ins
        mm_g1 = C("tensor", g1_mm, deps=[ld_x, ld_wg1, last_wr["GS_g1war"] if "GS_g1war" in last_wr else None])
        last_xfm_reader = mm_g1
        ev_g1 = C("scalar", lambda e: e.copy(out=G1b[:], in_=p[4][:, 0:GT]), deps=[mm_g1])
        gs_done = []
        for i in range(4):
            mm_ga = C("tensor", lambda e, i=i: e.matmul(
                p[5][:, 0:512], G1b[:, i * 128:(i + 1) * 128], wg2[:, 0:512], start=True, stop=True),
                deps=[ev_g1, ld_wg2, gs_done[-2] if len(gs_done) >= 2 else None])
            mm_gb = C("tensor", lambda e, i=i: e.matmul(
                p[4][:, 0:512], G1b[:, i * 128:(i + 1) * 128], wg2[:, 512:1024], start=True, stop=True),
                deps=[ev_g1, ld_wg2, gs_done[-1] if len(gs_done) >= 1 else None])
            sa = C("scalar", lambda e, i=i: e.activation(
                out=GSg[:, i * 1024: i * 1024 + 512], in_=p[5][:, 0:512], func=AF.Silu), deps=[mm_ga])
            sb_ = C("scalar", lambda e, i=i: e.activation(
                out=GSg[:, i * 1024 + 512: (i + 1) * 1024], in_=p[4][:, 0:512], func=AF.Silu), deps=[mm_gb])
            gs_done += [sa, sb_]
        wr_gs = DMA("sync", lambda e, t0=t0: e.dma_start(
            out=GSH[t0:t0 + GT, :].rearrange("(i p) c -> p i c", p=128),
            in_=GSg[:].rearrange("p (i c) -> p i c", i=4)), deps=gs_done)
        last_wr["GS"] = wr_gs
        last_wr["GS_g1war"] = gs_done[-1]

    a1_done = [last_wr[k] for k in last_wr]

    A1_MARK = REG.mark()
    REG.reset(A1_BASE)
    QPs = [sbR(f"QPs{k}", [128, T], BF16) for k in range(2)]
    EBs = [sbR(f"EBs{k}", [128, T], BF16) for k in range(2)]
    ETs = [sbR(f"ETs{k}", [128, T], BF16) for k in range(2)]
    VTs = [sbR(f"VTs{k}", [128, T], BF16) for k in range(2)]
    PMT = [sbR(f"PMT{k}", [128, 128], BF16) for k in range(4)]
    SBF = [sbR(f"SBF{k}", [128, 128], BF16) for k in range(2)]
    OST = [sbR(f"OST{k}", [128, 128], BF16) for k in range(4)]

    ot_wr = []
    prev_pair_state = []
    for pair in range(4):
        lds = []
        for k in range(2):
            h = 2 * pair + k
            lds.append((
                DMA("sync", lambda e, h=h, k=k: e.dma_start(out=QPs[k][:], in_=QPH[h]), deps=[last_wr["QP"]] + prev_pair_state),
                DMA("sync", lambda e, h=h, k=k: e.dma_start(out=EBs[k][:], in_=EBH[h]), deps=[last_wr["EB"]] + prev_pair_state),
                DMA("sync", lambda e, h=h, k=k: e.dma_start(
                    out=ETs[k][:].rearrange("p (i c) -> p i c", c=128),
                    in_=ETOKH[h].rearrange("(i p) c -> p i c", p=128)), deps=[last_wr["ET"]] + prev_pair_state),
                DMA("sync", lambda e, h=h, k=k: e.dma_start(
                    out=VTs[k][:].rearrange("p (i c) -> p i c", c=128),
                    in_=VTOKH[h].rearrange("(i p) c -> p i c", p=128)), deps=[last_wr["VT"]] + prev_pair_state),
            ))
        state = [None, None]
        o_mm_prev = [None, None]
        pmt_prev = {}
        ost_prev = {}
        for c in range(NT):
            for k in range(2):
                h = 2 * pair + k
                pS = p[4] if k == 0 else p[5]
                pPT = p[2] if (c % 2 == 0) else p[3]
                pO = p[0] if (c % 2 == 0) else p[1]
                pmt = PMT[2 * k + c % 2]
                ost = OST[2 * k + c % 2]
                sl = slice(c * 128, (c + 1) * 128)
                mm_pt = C("tensor", lambda e, k=k, sl=sl, pPT=pPT: e.matmul(
                    pPT[:, 0:128], EBs[k][:, sl], QPs[k][:, sl], start=True, stop=True),
                    deps=list(lds[k][:2]) + [pmt_prev.get((k, c % 2))])
                ev_pt = C("vector", lambda e, pmt=pmt, pPT=pPT: e.tensor_tensor(
                    pmt[:], pPT[:, 0:128], cMST[:], ALU.mult), deps=[mm_pt])
                pmt_prev[(k, c % 2)] = ev_pt

                def o_mms(e, k=k, sl=sl, pO=pO, pmt=pmt, c=c):
                    if c > 0:
                        e.matmul(pO[:, 0:128], QPs[k][:, sl], SBF[k][:], start=True, stop=False)
                    ins = e.matmul(pO[:, 0:128], pmt[:], VTs[k][:, sl], start=(c == 0), stop=True)
                    return ins
                mm_o = C("tensor", o_mms,
                         deps=[ev_pt, state[k], lds[k][3], ost_prev.get((k, c % 2))])
                mm_s = C("tensor", lambda e, k=k, sl=sl, pS=pS, c=c: e.matmul(
                    pS[:, 0:128], ETs[k][:, sl], VTs[k][:, sl], start=(c == 0), stop=(c == NT - 1)),
                    deps=[lds[k][2], lds[k][3], mm_o])
                ev_s = C("vector", lambda e, k=k, pS=pS: e.tensor_copy(out=SBF[k][:], in_=pS[:, 0:128]),
                         deps=[mm_s])
                state[k] = ev_s
                ev_o = C("vector", lambda e, ost=ost, pO=pO: e.tensor_copy(out=ost[:], in_=pO[:, 0:128]),
                         deps=[mm_o])
                wr_o = DMA("sync", lambda e, c=c, h=h, ost=ost: e.dma_start(
                    out=OTH[c * 128:(c + 1) * 128, h * 128:(h + 1) * 128], in_=ost[:]), deps=[ev_o])
                ost_prev[(k, c % 2)] = wr_o
                ot_wr.append(wr_o)
        if c == NT - 1:
            prev_pair_state = [state[0], state[1]] + ot_wr[-2:]

    a2_final = prev_pair_state + ot_wr[-1:]
    REG.reset(A1_BASE)
    wo = sbR("wo", [128, 8 * 1024], BF16)
    rw = sbR("rw", [128, 8 * E], BF16)
    gnwc = sbR("gnwc", [128, 8], F32)
    wof = sbR("wof", [128, 1024], F32)
    ld_gnw = DMA("sync", lambda e: e.dma_start(out=gnwc[:], in_=GNW[:]), deps=a2_final)
    ld_rw = DMA("gpsimd", lambda e: e.dma_start(
        out=rw[:].rearrange("p (c n) -> p c n", n=E),
        in_=RWT[:].rearrange("(c p) n -> p c n", p=128)), deps=a2_final)
    wo_ready = []
    ld_wo_f32 = []
    for dc in range(8):
        ldc = DMA("sync", lambda e, dc=dc: e.dma_start(out=wof[:], in_=WOT[dc * 128:(dc + 1) * 128, :]),
                  deps=[ld_gnw, wo_ready[-1] if wo_ready else None] + a2_final)
        sc = C("vector", lambda e, dc=dc: e.tensor_scalar_mul(
            wo[:, dc * 1024:(dc + 1) * 1024], wof[:], gnwc[:, dc:dc + 1]), deps=[ldc, ld_gnw])
        wo_ready.append(sc)

    OTt = sbR("OTt", [128, 1024], BF16)
    GSt = sbR("GSt", [128, 1024], BF16)
    XTt = sbR("XTt", [128, 1024], F32)
    SQ = sbR("SQ", [128, 1024], F32)
    COLS = sbR("COLS", [128, 8], F32)
    OF = sbR("OF", [128, 1024], BF16)
    OFM = sbR("OFM", [128, 1024], BF16)
    X1P = sbR("X1P", [128, 1024], F32)
    X1B = sbR("X1B", [128, 1024], BF16)
    X1F = sbR("X1F", [128, 1024], BF16)
    LOGb = sbR("LOGb", [128, E], F32)
    ELb = sbR("ELb", [128, E], F32)
    T8b = sbR("T8b", [128, 8], F32)
    WNb = sbR("WNb", [128, E], F32)
    Wtb = sbR("Wtb", [128, E], F32)
    S8b = sbR("S8b", [128, 8], F32)
    S8f = sbR("S8f", [128, 8], F32)
    S8u = sbR("S8u", [128, 8], U32)
    X1DBGs = sbR("X1DBGs", [128, 1024], F32) if stage == "x1" else None

    tp_prev = None
    prev_tile = {}
    for i in range(NT):
        t0 = i * 128
        dp = prev_tile
        first_deps = a2_final if i == 0 else []
        ld_ot = DMA("sync", lambda e, t0=t0: e.dma_start(out=OTt[:], in_=OTH[t0:t0 + 128, :]),
                    deps=(ot_wr[-1:] + a2_final) if i == 0 else [dp.get("otwar")])
        ld_gs = DMA("sync", lambda e, t0=t0: e.dma_start(out=GSt[:], in_=GSH[t0:t0 + 128, :]),
                    deps=[last_wr["GS"], dp.get("otwar")] + first_deps)
        ld_xt = DMA("sync", lambda e, t0=t0: e.dma_start(out=XTt[:], in_=XTOK[t0:t0 + 128, :]),
                    deps=[dp.get("x1p")] + first_deps)
        s_sq = C("scalar", lambda e: e.activation(out=SQ[:], in_=OTt[:], func=AF.Square),
                 deps=[ld_ot, dp.get("sq_done")])
        v_sq = C("vector", lambda e: e.reduce_sum(COLS[:, 0:1], SQ[:], axis=AX.X), deps=[s_sq])
        v_rc = C("vector", lambda e: e.tensor_scalar(COLS[:, 1:2], COLS[:, 0:1], 1.0 / D, 1e-5 * DK,
                                                     ALU.mult, ALU.add), deps=[v_sq])
        s_rc = C("scalar", lambda e: e.activation(out=COLS[:, 2:3], in_=COLS[:, 1:2], func=AF.Sqrt),
                 deps=[v_rc])
        v_rc2 = C("vector", lambda e: e.reciprocal(COLS[:, 3:4], COLS[:, 2:3]), deps=[s_rc])
        v_of = C("vector", lambda e: e.scalar_tensor_tensor(
            OF[:], OTt[:], COLS[:, 3:4], GSt[:], ALU.mult, ALU.mult), deps=[v_rc2, ld_gs])
        prev_tile_of = v_of
        if stage == "x1" and i == 4:
            DMA("sync", lambda e: e.dma_start(out=outs["DOF"][:], in_=OF[:]), deps=[v_of])
        trs = []
        for h in range(8):
            pt = pt6 if h % 2 == 0 else pt7
            tr = C("tensor", lambda e, h=h, pt=pt: e.transpose(
                pt[:, 0:128], OF[:, h * 128:(h + 1) * 128], cID[:]),
                deps=[v_of, ld_cid, trs[-2][1] if len(trs) >= 2 else dp.get("womm")])
            ev = C("scalar", lambda e, h=h, pt=pt: e.copy(
                out=OFM[:, h * 128:(h + 1) * 128], in_=pt[:, 0:128]), deps=[tr])
            trs.append((tr, ev))
        def wo_mm(e, half):
            for cc in range(8):
                ins = e.matmul(p[0 + half][:, 0:512], OFM[:, cc * 128:(cc + 1) * 128],
                               wo[:, cc * 1024 + half * 512: cc * 1024 + half * 512 + 512],
                               start=(cc == 0), stop=(cc == 7))
            return ins
        mm_wo0 = C("tensor", lambda e: wo_mm(e, 0), deps=[ev for _, ev in trs] + wo_ready + [dp.get("x1p")])
        mm_wo1 = C("tensor", lambda e: wo_mm(e, 1), deps=[mm_wo0])
        v_x1a = C("vector", lambda e: e.tensor_tensor(X1P[:, 0:512], p[0][:, 0:512], XTt[:, 0:512], ALU.add),
                  deps=[mm_wo0, ld_xt])
        v_x1b = C("vector", lambda e: e.tensor_tensor(X1P[:, 512:1024], p[1][:, 0:512], XTt[:, 512:1024], ALU.add),
                  deps=[mm_wo1, ld_xt])
        v_nm = C("vector", lambda e: e.tensor_reduce(COLS[:, 4:5], X1P[:], AX.X, ALU.add, negate=True),
                 deps=[v_x1a, v_x1b])
        v_nm2 = C("vector", lambda e: e.tensor_scalar_mul(COLS[:, 4:5], COLS[:, 4:5], 1.0 / D), deps=[v_nm])
        s_sq2 = C("scalar", lambda e: e.activation(out=SQ[:], in_=X1P[:], func=AF.Square),
                  deps=[v_x1a, v_x1b, s_sq, v_sq])
        v_sq2 = C("vector", lambda e: e.reduce_sum(COLS[:, 5:6], SQ[:], axis=AX.X), deps=[s_sq2])
        sq_done = v_sq2
        v_m2 = C("vector", lambda e: e.tensor_tensor(COLS[:, 6:7], COLS[:, 4:5], COLS[:, 4:5], ALU.mult),
                 deps=[v_nm2])
        v_var = C("vector", lambda e: e.tensor_scalar(COLS[:, 5:6], COLS[:, 5:6], 1.0 / D, 1e-5,
                                                      ALU.mult, ALU.add), deps=[v_sq2])
        v_var2 = C("vector", lambda e: e.tensor_sub(COLS[:, 5:6], COLS[:, 5:6], COLS[:, 6:7]),
                   deps=[v_var, v_m2])
        s_sd = C("scalar", lambda e: e.activation(out=COLS[:, 6:7], in_=COLS[:, 5:6], func=AF.Sqrt),
                 deps=[v_var2])
        v_rsd = C("vector", lambda e: e.reciprocal(COLS[:, 7:8], COLS[:, 6:7]), deps=[s_sd])
        if stage == "x1" and i == 4:
            DMA("sync", lambda e: e.dma_start(out=outs["DX1P"][:], in_=X1P[:]), deps=[v_x1a, v_x1b, v_rsd])
            DMA("sync", lambda e: e.dma_start(out=outs["DCOL"][:], in_=COLS[:]), deps=[v_rsd])
        v_x1 = C("vector", lambda e: e.scalar_tensor_tensor(
            X1B[:], X1P[:], COLS[:, 4:5], COLS[:, 7:8].broadcast_to([128, 1024]), ALU.add, ALU.mult),
            deps=[v_rsd, dp.get("x1war"), dp.get("x1wr")])
        if stage == "x1":
            v_x1f = C("vector", lambda e: e.scalar_tensor_tensor(
                X1DBGs[:], X1P[:], COLS[:, 4:5], COLS[:, 7:8].broadcast_to([128, 1024]), ALU.add, ALU.mult),
                deps=[v_rsd])
            DMA("sync", lambda e, t0=t0: e.dma_start(out=outs["X1DBG"][t0:t0 + 128, :], in_=X1DBGs[:]),
                deps=[v_x1f])
        wr_x1 = DMA("sync", lambda e, t0=t0: e.dma_start(out=X1TOK[t0:t0 + 128, :], in_=X1B[:]), deps=[v_x1])
        prev_x1wr = wr_x1
        trs2 = []
        for dc in range(8):
            pt = pt6 if dc % 2 == 0 else pt7
            tr = C("tensor", lambda e, dc=dc, pt=pt: e.transpose(
                pt[:, 0:128], X1B[:, dc * 128:(dc + 1) * 128], cID[:]),
                deps=[v_x1, ld_cid, trs2[-2][1] if len(trs2) >= 2 else (trs[-1][0] if trs else None),
                      dp.get("x1fwr") if dc == 0 else None])
            ev = C("scalar", lambda e, dc=dc, pt=pt: e.copy(
                out=X1F[:, dc * 128:(dc + 1) * 128], in_=pt[:, 0:128]), deps=[tr])
            trs2.append((tr, ev))
        wr_x1f = DMA("sync", lambda e, t0=t0: e.dma_start(
            out=X1FMH[:, :, t0:t0 + 128].rearrange("c p t -> p c t"), in_=X1F[:].rearrange("p (c t) -> p c t", t=128)),
            deps=[ev for _, ev in trs2])
        def r_mm(e):
            for dc in range(8):
                ins = e.matmul(p[2][:, 0:E], X1F[:, dc * 128:(dc + 1) * 128], rw[:, dc * E:(dc + 1) * E],
                               start=(dc == 0), stop=(dc == 7))
            return ins
        mm_r = C("tensor", r_mm, deps=[ev for _, ev in trs2] + [ld_rw, dp.get("log")])
        v_log = C("vector", lambda e: e.tensor_copy(out=LOGb[:], in_=p[2][:, 0:E]), deps=[mm_r])
        if stage == "x1":
            DMA("sync", lambda e, t0=t0: e.dma_start(out=outs["LOGDBG"][t0:t0 + 128, :], in_=LOGb[:]),
                deps=[v_log])
        v_mx = C("vector", lambda e: e.tensor_reduce(COLS[:, 0:1], LOGb[:], AX.X, ALU.max, negate=True),
                 deps=[v_log, s_sq])
        s_el = C("scalar", lambda e: e.activation(out=ELb[:], in_=LOGb[:], func=AF.Exp,
                                                  bias=COLS[:, 0:1]), deps=[v_mx])
        v_t8 = C("vector", lambda e: e.max(T8b[:], LOGb[:]), deps=[v_log])
        v_ms = C("vector", lambda e, i=i: e.tensor_single_scalar(
            MSK[:, i * E:(i + 1) * E], LOGb[:], T8b[:, 7:8], ALU.is_ge), deps=[v_t8])
        v_wn = C("vector", lambda e, i=i: e.tensor_tensor(WNb[:], ELb[:], MSK[:, i * E:(i + 1) * E], ALU.mult),
                 deps=[s_el, v_ms])
        v_s = C("vector", lambda e: e.tensor_reduce(COLS[:, 1:2], WNb[:], AX.X, ALU.add), deps=[v_wn])
        v_sr = C("vector", lambda e: e.reciprocal(COLS[:, 2:3], COLS[:, 1:2]), deps=[v_s])
        v_w = C("vector", lambda e: e.tensor_scalar_mul(Wtb[:], WNb[:], COLS[:, 2:3]), deps=[v_sr, dp.get("wwar")])
        wr_w = DMA("sync", lambda e, t0=t0: e.dma_start(
            out=LOCTBL[CAP * E + 128 + t0 * E: CAP * E + 128 + (t0 + 128) * E, 0:1]
                .rearrange("(p e) c -> p (e c)", e=E), in_=Wtb[:]), deps=[v_w])
        mm_cnt = C("tensor", lambda e, i=i: (
            e.matmul(p[3][:, 0:E], cU[:], MSK[:, i * E:(i + 1) * E], start=True, stop=False),
            e.matmul(p[3][:, 0:E], cONEROW[:], TPROW[:], start=False, stop=True))[-1],
            deps=[v_ms, c_ones, tp_prev or c_tprow, dp.get("cnt_ev")])
        v_cnt = C("vector", lambda e, i=i: e.tensor_tensor(
            CNT[:, i * E:(i + 1) * E], p[3][:, 0:E], MSK[:, i * E:(i + 1) * E], ALU.subtract),
            deps=[mm_cnt])
        mm_tp = C("tensor", lambda e, i=i: e.matmul(
            p[5][0:1, 0:E], cONECOL[:], MSK[:, i * E:(i + 1) * E], start=(i == 0), stop=(i == NT - 1)),
            deps=[v_ms, c_ones2, mm_cnt])
        v_tp = C("vector", lambda e: e.tensor_copy(out=TPROW[:], in_=p[5][0:1, 0:E]), deps=[mm_tp])
        tp_prev = v_tp
        v_v1 = C("vector", lambda e, i=i: e.tensor_single_scalar(WNb[:], CNT[:, i * E:(i + 1) * E],
                                                                 511.5, ALU.is_lt), deps=[v_cnt, v_w])
        v_v2 = C("vector", lambda e, i=i: e.tensor_tensor(WNb[:], WNb[:], MSK[:, i * E:(i + 1) * E],
                                                          ALU.mult), deps=[v_v1])
        v_t1 = C("vector", lambda e, i=i: e.tensor_tensor(ELb[:], CNT[:, i * E:(i + 1) * E], cIOTAE[:],
                                                          ALU.add), deps=[v_cnt, s_el])
        v_t2 = C("vector", lambda e: e.tensor_tensor(ELb[:], ELb[:], WNb[:], ALU.mult), deps=[v_t1, v_v2])
        v_sx = C("vector", lambda e: e.tensor_scalar_add(ELb[:], ELb[:], -1.0), deps=[v_t2])
        v_m8 = C("vector", lambda e: e.max(S8b[:], ELb[:]), deps=[v_sx])
        v_oo = C("vector", lambda e: e.tensor_single_scalar(S8f[:], S8b[:], 0.0, ALU.is_lt), deps=[v_m8])
        v_fx = C("vector", lambda e: e.scalar_tensor_tensor(S8f[:], S8f[:], 2.0e9, S8b[:],
                                                            ALU.mult, ALU.add), deps=[v_oo])
        v_cu = C("vector", lambda e: e.tensor_copy(out=S8u[:], in_=S8f[:]), deps=[v_fx, dp.get("s8war")])
        sc_last = None
        for j in range(8):
            sc_last = DMA("gpsimd", lambda e, j=j, i=i: e.indirect_dma_start(
                out=LOCTBL[:, :], out_offset=IndirectOffsetOnAxis(S8u[:, j:j + 1], 0),
                in_=cTOKID[:, i:i + 1], in_offset=None,
                bounds_check=rB32K, oob_is_err=False), deps=[v_cu, z_loc, ld_consts[3]])
        prev_tile = {"of": trs[-1][0], "x1p": v_x1, "womm": mm_wo1, "log": v_log,
                     "cnt_ev": v_cnt, "sq_done": s_sq2, "scat": sc_last,
                     "otwar": v_of, "x1war": trs2[-1][0], "wwar": wr_w,
                     "s8war": sc_last, "elwar": v_sx, "x1wr": prev_x1wr, "x1fwr": wr_x1f}

    a3_final = [prev_tile[k] for k in prev_tile if prev_tile[k] is not None] + [tp_prev]
    if stage == "x1":
        DMA("sync", lambda e: e.dma_start(out=outs["DQP"][:], in_=QPH[0]), deps=[last_wr["QP"]])
        DMA("sync", lambda e: e.dma_start(out=outs["DEB"][:], in_=EBH[0]), deps=[last_wr["EB"]])
        DMA("sync", lambda e: e.dma_start(out=outs["DET"][:], in_=ETOKH[0, 0:GT, :]), deps=[last_wr["ET"]])
        DMA("sync", lambda e: e.dma_start(out=outs["DVT"][:], in_=VTOKH[0, 0:GT, :]), deps=[last_wr["VT"]])
        DMA("sync", lambda e: e.dma_start(out=outs["DOT"][:], in_=OTH[0:128, :]), deps=ot_wr[-1:])
        DMA("sync", lambda e: e.dma_start(out=outs["DGS"][:], in_=GSH[0:128, :]), deps=[last_wr["GS"]])
        return nc, P

    S("gpsimd", lambda e: None, inc=0, deps=[])
    DMAq = DMA
    wr_cnts = DMA("sync", lambda e: e.dma_start(
        out=LOCTBL[CAP * E:CAP * E + E, 0:1].rearrange("(a b) c -> a (b c)", a=1),
        in_=TPROW[:, :]), deps=[tp_prev])
    cc_loc = C("gpsimd", lambda e: e.collective_compute(
        "AllGather", ALU.bypass, replica_groups=[list(range(8))],
        ins=[LOCTBL.opt()], outs=[LOCAG.opt()]),
        deps=[wr_cnts, prev_tile["scat"], prev_tile["wwar"]])
    cc_x1 = C("gpsimd", lambda e: e.collective_compute(
        "AllGather", ALU.bypass, replica_groups=[list(range(8))],
        ins=[X1TOK.opt()], outs=[X1AG.opt()]),
        deps=[prev_tile["x1wr"], cc_loc])
    cc_w = cc_loc

    def dummy_out(dep):
        last = None
        for i2 in range(NT):
            last = DMA("sync", lambda e, i2=i2: e.dma_start(
                out=outs["OUT"][i2 * 128:(i2 + 1) * 128, :], in_=XTOK[i2 * 128:(i2 + 1) * 128, :]),
                deps=[dep] if i2 == 0 else [])
        S("sync", lambda e: None, inc=0, deps=[last])
    if stage == "moe1":
        dummy_out(cc_w)
        return nc, P

    REG.reset(A1_BASE)
    sCAG = sbR("sCAG", [8, E], F32)
    sCAGT = sbR("sCAGT", [E, 8], F32)
    sMyBT = sbR("sMyBT", [8, 8], F32)
    sPfx = sbR("sPfx", [8, 8], F32)
    sPexp = sbR("sPexp", [32, 8], F32)
    sCexp = sbR("sCexp", [32, 8], F32)
    sAccC = sbR("sAccC", [32, 128], F32)
    sAccB = sbR("sAccB", [32, 128], F32)
    sTMP = sbR("sTMP", [32, 128], F32)
    sFIDX = sbR("sFIDX", [128, 32], F32)
    sFIDXU = sbR("sFIDXU", [128, 32], U32)
    sGL = sbR("sGL", [128, 32], F32)
    sXIDXU = sbR("sXIDXU", [128, 32], U32)
    sWIDXU = sbR("sWIDXU", [128, 32], U32)
    sWCOL = sbR("sWCOL", [128, 32], F32)
    wgb = [sbR(f"wgb{k}", [128, 8 * FF], BF16) for k in range(2)]
    wub = [sbR(f"wub{k}", [128, 8 * FF], BF16) for k in range(2)]
    wdb = [sbR(f"wdb{k}", [128, 6 * 1024], BF16) for k in range(2)]
    XBT = sbR("XBT", [128, 4 * 1024], BF16)
    XBF = sbR("XBF", [128, 8 * 512], BF16)
    GSb = sbR("GSb", [128, 6 * 512], BF16)
    USb = sbR("USb", [128, 6 * 512], BF16)
    HSb = sbR("HSb", [128, 6 * 512], BF16)
    YST = [sbR(f"YST{k}", [128, 1024], BF16) for k in range(2)]
    SQdbg = sbR("SQdbg", [128, 512], F32)

    ld_cag = DMA("sync", lambda e: e.dma_start(
        out=sCAG[:], in_=LOCAG[:, 0:1].rearrange("(c r) one -> c (r one)", r=CAP * E + 128 + T * E)[:, CAP * E:CAP * E + E]),
        deps=[cc_loc] + a3_final)
    mm_br = C("tensor", lambda e: e.matmul(p[2][0:1, 0:E], cPREM[:], sCAG[:], start=True, stop=True),
              deps=[ld_cag, ld_consts[4]])
    v_br = C("vector", lambda e: e.tensor_copy(out=BASER[:], in_=p[2][0:1, 0:E]), deps=[mm_br])
    mm_bb = C("tensor", lambda e: e.matmul(p[3][:, 0:E], cONEROW[:], BASER[:], start=True, stop=True),
              deps=[v_br, c_ones])
    v_bb = C("vector", lambda e: e.tensor_copy(out=BASEBC[:], in_=p[3][:, 0:E]), deps=[mm_bb])
    hprev = {}
    for gi in range(NT):
        par = gi % 2
        vgp = C("vector", lambda e, gi=gi, par=par: e.tensor_tensor(
            HGP[par][:], CNT[:, gi * E:(gi + 1) * E], BASEBC[:], ALU.add),
            deps=[v_bb, hprev.get(("w", par))])
        vv1 = C("vector", lambda e, par=par: e.tensor_single_scalar(
            HT1[par][:], HGP[par][:], 511.5, ALU.is_lt), deps=[vgp])
        vv2 = C("vector", lambda e, gi=gi, par=par: e.tensor_tensor(
            HT1[par][:], HT1[par][:], MSK[:, gi * E:(gi + 1) * E], ALU.mult), deps=[vv1])
        vt1 = C("vector", lambda e, par=par: e.tensor_tensor(
            HGP[par][:], HGP[par][:], cIOTAE[:], ALU.add), deps=[vv2])
        vt2 = C("vector", lambda e, par=par: e.tensor_tensor(
            HGP[par][:], HGP[par][:], HT1[par][:], ALU.mult), deps=[vt1])
        vt3 = C("vector", lambda e, par=par: e.tensor_scalar_add(
            HGP[par][:], HGP[par][:], -1.0), deps=[vt2])
        vm8 = C("vector", lambda e, par=par: e.max(HS8[par][:], HGP[par][:]), deps=[vt3])
        vo1 = C("vector", lambda e, par=par: e.tensor_single_scalar(
            HT1[par][:, 0:8], HS8[par][:], 0.0, ALU.is_lt), deps=[vm8])
        vo2 = C("vector", lambda e, par=par: e.scalar_tensor_tensor(
            HS8[par][:], HT1[par][:, 0:8], 2.0e9, HS8[par][:], ALU.mult, ALU.add), deps=[vo1])
        vo3 = C("vector", lambda e, gi=gi, par=par: e.tensor_copy(
            out=S8ALL[:, gi * 8:(gi + 1) * 8], in_=HS8[par][:]), deps=[vo2])
        hprev[("w", par)] = vo3
    s8_done = hprev[("w", 0)], hprev[("w", 1)]

    mm_ct = C("tensor", lambda e: e.transpose(p[4][0:E, 0:8], sCAG[:], cIDF[0:8, 0:8]),
              deps=[ld_cag, ld_consts[1], mm_bb])
    v_ct = C("vector", lambda e: e.tensor_copy(out=sCAGT[:], in_=p[4][0:E, 0:8]), deps=[mm_ct])
    mm_mb = C("tensor", lambda e: e.matmul(p[2][0:8, 0:8], cSEL[:], sCAGT[:], start=True, stop=True),
              deps=[v_ct, ld_consts[5], v_br])
    v_mb = C("vector", lambda e: e.tensor_copy(out=sMyBT[:], in_=p[2][0:8, 0:8]), deps=[mm_mb])
    v_px = C("vector", lambda e: e.tensor_tensor_scan(sPfx[:], sMyBT[:], cZ[0:8, 0:8],
                                                      0.0, ALU.add, ALU.add), deps=[v_mb])
    v_px2 = C("vector", lambda e: e.tensor_sub(sPfx[:], sPfx[:], sMyBT[:]), deps=[v_px])
    mm_pe = C("tensor", lambda e: e.matmul(p[3][0:32, 0:8], cEXP48[:], sPfx[:], start=True, stop=True),
              deps=[v_px2, ld_consts[6], v_bb])
    v_pe = C("vector", lambda e: e.tensor_copy(out=sPexp[:], in_=p[3][0:32, 0:8]), deps=[mm_pe])
    mm_ce = C("tensor", lambda e: e.matmul(p[2][0:32, 0:8], cEXP48[:], sMyBT[:], start=True, stop=True),
              deps=[v_mb, mm_pe])
    v_ce = C("vector", lambda e: e.tensor_copy(out=sCexp[:], in_=p[2][0:32, 0:8]), deps=[mm_ce])
    if stage == "moe2a":
        dummy_out(v_ce)
        return nc, P
    m_ac = C("vector", lambda e: e.memset(sAccC[:], 0.0), deps=a3_final)
    m_ab = C("vector", lambda e: e.memset(sAccB[:], 0.0), deps=a3_final)
    accc, accb = m_ac, m_ab
    for cc_ in range(1, 8):
        ge = C("vector", lambda e, cc_=cc_: e.tensor_single_scalar(
            sTMP[:], cPVAL[:], sPexp[:, cc_:cc_ + 1], ALU.is_ge), deps=[v_pe, accb, ld_consts[7]])
        a1_ = C("vector", lambda e: e.tensor_add(sAccC[:], sAccC[:], sTMP[:]), deps=[ge, accc])
        a2_ = C("vector", lambda e, cc_=cc_: e.scalar_tensor_tensor(
            sAccB[:], sTMP[:], sCexp[:, cc_ - 1:cc_], sAccB[:], ALU.mult, ALU.add),
            deps=[ge, v_ce, a1_])
        accc, accb = a1_, a2_
    f1 = C("vector", lambda e: e.tensor_scalar(sTMP[:], sAccC[:], float(CAP * E + 128 + T * E), 0.0,
                                               ALU.mult, ALU.add), deps=[accc, accb])
    f2 = C("vector", lambda e: e.tensor_add(sTMP[:], sTMP[:], cPVAL[:]), deps=[f1])
    f3 = C("vector", lambda e: e.tensor_sub(sTMP[:], sTMP[:], sAccB[:]), deps=[f2])
    f4 = C("vector", lambda e: e.tensor_scalar_add(sTMP[:], sTMP[:], cEOFF[:, 0:1]), deps=[f3, ld_consts[8]])
    mm_ft = C("tensor", lambda e: e.transpose(p[5][:, 0:32].bitcast(F32), sTMP[:], cIDF[0:32, 0:32]),
              deps=[f4, mm_ce])
    v_ft = C("vector", lambda e: e.tensor_copy(out=sFIDX[:], in_=p[5][:, 0:32].bitcast(F32)), deps=[mm_ft])
    v_fu = C("vector", lambda e: e.tensor_copy(out=sFIDXU[:], in_=sFIDX[:]), deps=[v_ft])
    if stage == "moe2b":
        dummy_out(v_fu)
        return nc, P
    gl_done = []
    for r_ in range(32):
        gstep = DMA("gpsimd", lambda e, r_=r_: e.indirect_dma_start(
            out=sGL[:, r_:r_ + 1], out_offset=None,
            in_=LOCAG[:, :], in_offset=IndirectOffsetOnAxis(sFIDXU[:, r_:r_ + 1], 0),
            bounds_check=rB256K, oob_is_err=False), deps=[v_fu, cc_x1])
        gl_done.append(gstep)
    if stage == "moe2c":
        dummy_out(gl_done[-1])
        return nc, P
    sOO = sbR("sOO", [128, 32], F32)
    OOB32 = sbR("OOB32", [128, 32], F32)
    v_oo = C("vector", lambda e: e.tensor_single_scalar(sOO[:], sGL[:], 0.5, ALU.is_lt),
             deps=gl_done[-1:])
    v_ob = C("vector", lambda e: e.tensor_scalar_mul(OOB32[:], sOO[:], 2.0e9),
             deps=[v_oo])
    v_x1i = C("vector", lambda e: e.tensor_scalar_add(sGL[:], sGL[:], -1.0), deps=[v_ob] + gl_done)
    v_x2i = C("vector", lambda e: e.tensor_add(sFIDX[:], sGL[:], OOB32[:]), deps=[v_x1i])
    v_x3i = C("vector", lambda e: e.tensor_copy(out=sXIDXU[:], in_=sFIDX[:]), deps=[v_x2i])
    v_w1 = C("vector", lambda e: e.tensor_scalar(sFIDX[:], sGL[:], 64.0, 0.0, ALU.mult, ALU.add),
             deps=[v_x3i])
    v_w2 = C("vector", lambda e: e.tensor_add(sFIDX[:], sFIDX[:], cECON[:]), deps=[v_w1, ld_consts[9]])
    sCC = sbR("sCC", [128, 32], F32)
    v_wc0 = C("vector", lambda e: e.memset(sCC[:], 0.0), deps=[v_ct])
    _pw = v_wc0
    for cc2 in range(1, 8):
        _pw = C("vector", lambda e, cc2=cc2: e.tensor_single_scalar(
            OOB32[:], sGL[:], cc2 * 4096.0 - 0.5, ALU.is_ge), deps=[v_x1i, _pw])
        _pw = C("vector", lambda e: e.tensor_add(sCC[:], sCC[:], OOB32[:]), deps=[_pw])
    v_w2b = C("vector", lambda e: e.scalar_tensor_tensor(
        sFIDX[:], sCC[:], float(CAP * E + 128), sFIDX[:], ALU.mult, ALU.add),
        deps=[v_w2, _pw])
    v_w2c = C("vector", lambda e: e.tensor_scalar_add(sFIDX[:], sFIDX[:], float(CAP * E + 128)),
              deps=[v_w2b])
    v_ob2 = C("vector", lambda e: e.tensor_scalar_mul(OOB32[:], sOO[:], 2.0e9), deps=[v_w2c])
    v_w3 = C("vector", lambda e: e.tensor_add(sFIDX[:], sFIDX[:], OOB32[:]), deps=[v_ob2])
    v_w4 = C("vector", lambda e: e.tensor_copy(out=sWIDXU[:], in_=sFIDX[:]), deps=[v_w3])
    if stage == "moe2d":
        dummy_out(v_w4)
        return nc, P
    m_wc = C("vector", lambda e: e.memset(sWCOL[:], 0.0), deps=[v_ct])
    w_done = []
    for r_ in range(32):
        wstep = DMA("gpsimd", lambda e, r_=r_: e.indirect_dma_start(
            out=sWCOL[:, r_:r_ + 1], out_offset=None,
            in_=LOCAG[:, :],
            in_offset=IndirectOffsetOnAxis(sWIDXU[:, r_:r_ + 1], 0),
            bounds_check=rB2M, oob_is_err=False), deps=[v_w4, m_wc, cc_x1])
        w_done.append(wstep)

    if stage == "moe2":
        dummy_out(w_done[-1])
        return nc, P
    yb_wr = []
    prev_exp = {}
    for j in range(EPC):
        par = j % 2
        dep_w = prev_exp.get(("wload", par))
        lw1 = DMA("gpsimd", lambda e, j=j, par=par: e.dma_start(
            out=wgb[par][:].rearrange("p (c n) -> p c n", n=FF),
            in_=WGE[j].rearrange("(c p) n -> p c n", p=128)), deps=[dep_w] + (a3_final if j < 2 else []))
        lw2 = DMA("gpsimd", lambda e, j=j, par=par: e.dma_start(
            out=wub[par][:].rearrange("p (c n) -> p c n", n=FF),
            in_=WUE[j].rearrange("(c p) n -> p c n", p=128)), deps=[lw1])
        lw3 = DMA("gpsimd", lambda e, j=j, par=par: e.dma_start(
            out=wdb[par][:].rearrange("p (c n) -> p c n", n=1024),
            in_=WDE[j].rearrange("(c p) n -> p c n", p=128)), deps=[lw2])
        mz = C("vector", lambda e: e.memset(XBT[:], 0.0), deps=[prev_exp.get("xbf_done"), v_x3i])
        gx = []
        for pc in range(4):
            gx.append(DMA("gpsimd", lambda e, j=j, pc=pc: e.indirect_dma_start(
                out=XBT[:, pc * 1024:(pc + 1) * 1024],
                out_offset=None,
                in_=X1AG[:, :], in_offset=IndirectOffsetOnAxis(sXIDXU[:, 4 * j + pc:4 * j + pc + 1], 0),
                bounds_check=rB32K, oob_is_err=False), deps=[mz, cc_x1, lw3]))
        trx = []
        for pc in range(4):
            for dc in range(8):
                pt = pt6 if (pc * 8 + dc) % 2 == 0 else pt7
                tr = C("tensor", lambda e, pc=pc, dc=dc, pt=pt: e.transpose(
                    pt[:, 0:128], XBT[:, pc * 1024 + dc * 128: pc * 1024 + (dc + 1) * 128], cID[:]),
                    deps=[gx[pc], trx[-2][1] if len(trx) >= 2 else prev_exp.get("gu_done"), ld_cid])
                ev = C("scalar", lambda e, pc=pc, dc=dc, pt=pt: e.copy(
                    out=XBF[:, dc * 512 + pc * 128: dc * 512 + (pc + 1) * 128], in_=pt[:, 0:128]),
                    deps=[tr])
                trx.append((tr, ev))
        xbf_done = [ev for _, ev in trx]
        h_done = []
        for fc in range(6):
            def g_mm(e, fc=fc, par=par):
                for dc in range(8):
                    ins = e.matmul(p[0][:, 0:512], wgb[par][:, dc * FF + fc * 128: dc * FF + (fc + 1) * 128],
                                   XBF[:, dc * 512:(dc + 1) * 512], start=(dc == 0), stop=(dc == 7))
                return ins
            mmg = C("tensor", g_mm, deps=xbf_done + [lw1, prev_exp.get(("evg", fc))])
            evg = C("scalar", lambda e, fc=fc: e.activation(out=GSb[:, fc * 512:(fc + 1) * 512],
                                                            in_=p[0][:, 0:512], func=AF.Silu), deps=[mmg])
            def u_mm(e, fc=fc, par=par):
                for dc in range(8):
                    ins = e.matmul(p[1][:, 0:512], wub[par][:, dc * FF + fc * 128: dc * FF + (fc + 1) * 128],
                                   XBF[:, dc * 512:(dc + 1) * 512], start=(dc == 0), stop=(dc == 7))
                return ins
            mmu = C("tensor", u_mm, deps=xbf_done + [lw2, prev_exp.get(("evu", fc)), evg])
            evu = C("scalar", lambda e, fc=fc: e.copy(out=USb[:, fc * 512:(fc + 1) * 512],
                                                      in_=p[1][:, 0:512]), deps=[mmu])
            hh = C("vector", lambda e, fc=fc: e.tensor_tensor(
                HSb[:, fc * 512:(fc + 1) * 512], GSb[:, fc * 512:(fc + 1) * 512],
                USb[:, fc * 512:(fc + 1) * 512], ALU.mult), deps=[evu, evg, prev_exp.get(("hh", fc))])
            prev_exp[("evg", fc)] = evg
            prev_exp[("evu", fc)] = evu
            prev_exp[("hh", fc)] = hh
            h_done.append(hh)
        ywr = []
        for tg in range(4):
            def y_mm(e, tg=tg, par=par, half=0):
                for fc in range(6):
                    ins = e.matmul(p[2][:, 0:512], HSb[:, fc * 512 + tg * 128: fc * 512 + (tg + 1) * 128],
                                   wdb[par][:, fc * 1024: fc * 1024 + 512], start=(fc == 0), stop=(fc == 5))
                return ins
            def y_mm2(e, tg=tg, par=par):
                for fc in range(6):
                    ins = e.matmul(p[3][:, 0:512], HSb[:, fc * 512 + tg * 128: fc * 512 + (tg + 1) * 128],
                                   wdb[par][:, fc * 1024 + 512: (fc + 1) * 1024], start=(fc == 0), stop=(fc == 5))
                return ins
            mmy1 = C("tensor", y_mm, deps=h_done + [lw3, prev_exp.get(("ye", tg, 0))])
            if stage == "full_dbg" and j == 0 and tg == 0:
                dyp = C("vector", lambda e: e.tensor_copy(out=SQdbg[:], in_=p[2][:, 0:512]), deps=[mmy1])
                DMA("sync", lambda e: e.dma_start(out=outs["DYP"][:], in_=SQdbg[:]), deps=[dyp])
            mmy2 = C("tensor", y_mm2, deps=h_done + [lw3, prev_exp.get(("ye", tg, 1))])
            yp = (4 * j + tg) % 2
            ev1 = C("vector", lambda e, j=j, tg=tg, yp=yp: e.tensor_scalar_mul(
                YST[yp][:, 0:512], p[2][:, 0:512], sWCOL[:, 4 * j + tg:4 * j + tg + 1]),
                deps=[mmy1] + w_done[-1:] + [prev_exp.get(("ystwr", yp))])
            ev2 = C("vector", lambda e, j=j, tg=tg, yp=yp: e.tensor_scalar_mul(
                YST[yp][:, 512:1024], p[3][:, 0:512], sWCOL[:, 4 * j + tg:4 * j + tg + 1]),
                deps=[mmy2] + w_done[-1:])
            wr = DMA("sync", lambda e, j=j, tg=tg, yp=yp: e.dma_start(
                out=YB[j * CAP + tg * 128: j * CAP + (tg + 1) * 128, :], in_=YST[yp][:]),
                deps=[ev1, ev2])
            prev_exp[("ystwr", yp)] = wr

            prev_exp[("ye", tg, 0)] = ev1
            prev_exp[("ye", tg, 1)] = ev2
            prev_exp[("yst", tg)] = wr
            ywr.append(wr)
        if stage == "full_dbg" and j == 0:
            dx1 = DMA("sync", lambda e: e.dma_start(out=outs["DXB"][:], in_=XBT[:]), deps=gx)
            dx2 = DMA("sync", lambda e: e.dma_start(out=outs["DXBF"][:], in_=XBF[:]), deps=xbf_done)
            dx3 = DMA("sync", lambda e: e.dma_start(out=outs["DGSB"][:], in_=GSb[:]), deps=h_done)
            dx4 = DMA("sync", lambda e: e.dma_start(out=outs["DHSB"][:], in_=HSb[:]), deps=h_done)
            ywr.append(dx1); ywr.append(dx2); ywr.append(dx3); ywr.append(dx4)
        prev_exp[("wload", par)] = ywr[-1]
        prev_exp["xbf_done"] = xbf_done[-1]
        prev_exp["gu_done"] = h_done[-1]
        yb_wr += ywr
    dmp1 = DMA("sync", lambda e: e.dma_start(out=outs["DGL"][:], in_=sGL[:]), deps=[v_x1i]) if stage == "full_dbg" else None
    dmp2 = DMA("sync", lambda e: e.dma_start(out=outs["DWC"][:], in_=sWCOL[:]), deps=w_done) if stage == "full_dbg" else None
    dmp3 = DMA("sync", lambda e: e.dma_start(out=outs["DXI"][:], in_=sXIDXU[:]), deps=[v_x3i]) if stage == "full_dbg" else None
    b_final = yb_wr[-4:] + [prev_exp[("hh", fc)] for fc in range(6)] + [d_ for d_ in (dmp1, dmp2, dmp3) if d_]

    cc_y = C("gpsimd", lambda e: e.collective_compute(
        "AllGather", ALU.bypass, replica_groups=[list(range(8))],
        ins=[YB.opt()], outs=[YAG.opt()]), deps=yb_wr)

    if stage == "moe3":
        dummy_out(cc_y)
        return nc, P
    REG.reset(A1_BASE)
    mzpre = {}
    wsg = sbR("wsg", [128, 8 * 1536], BF16)
    wsd = sbR("wsd", [128, 6 * 1024], BF16)
    X1F2 = sbR("X1F2", [128, 4096], BF16)
    SGb = sbR("SGb", [128, 6 * 512], BF16)
    SUb = sbR("SUb", [128, 6 * 512], BF16)
    HSh = sbR("HSh", [128, 6 * 512], BF16)
    Yg = [sbR(f"Yg{k}", [128, 8 * 1024], BF16) for k in range(2)] + [
        nc.alloc_sbuf_tensor_at("Yg2", [128, 8 * 1024], BF16, offset=MSK_OFF)]
    X1B2 = [sbR(f"X1B2{k}", [128, 1024], BF16) for k in range(2)]
    C1b = [sbR(f"C1b{k}", [128, 1024], F32) for k in range(2)]
    SQ2 = [sbR(f"SQ2{k}", [128, 1024], F32) for k in range(2)]
    CL2 = [sbR(f"CL2{k}", [128, 8], F32) for k in range(2)]
    OUTt = [sbR(f"OUTt{k}", [128, 1024], F32) for k in range(2)]

    lws1 = DMA("gpsimd", lambda e: e.dma_start(
        out=wsg[:].rearrange("p (c n) -> p c n", n=1536),
        in_=WSGU[:].rearrange("(c p) n -> p c n", p=128)), deps=b_final)
    lws2 = DMA("gpsimd", lambda e: e.dma_start(
        out=wsd[:].rearrange("p (c n) -> p c n", n=1024),
        in_=WSD[:].rearrange("(c p) n -> p c n", p=128)), deps=[lws1])
    out_wr = []
    prevc = {}
    for g in range(NG):
        t0g = g * GT
        ldx1f = DMA("sync", lambda e, t0g=t0g: e.dma_start(
            out=X1F2[:].rearrange("p (c t) -> p c t", t=GT),
            in_=X1FMH[:, :, t0g:t0g + GT].rearrange("c p t -> p c t")),
            deps=[prev_tile["x1fwr"], prevc.get("gu_last")] + (b_final if g == 0 else []))
        hs_done = []
        for fc in range(12):
            def gu_mm(e, fc=fc):
                for dc in range(8):
                    ins = e.matmul(p[0][:, 0:512], wsg[:, dc * 1536 + fc * 128: dc * 1536 + (fc + 1) * 128],
                                   X1F2[:, dc * GT:(dc + 1) * GT], start=(dc == 0), stop=(dc == 7))
                return ins
            mmgu = C("tensor", gu_mm, deps=[ldx1f, lws1, prevc.get(("sev", fc)),
                                            hs_done[fc - 1] if fc > 0 else prevc.get("odd_rout"),
                                            prevc.get("gu_last")])
            if fc < 6:
                sev = C("scalar", lambda e, fc=fc: e.activation(
                    out=SGb[:, fc * 512:(fc + 1) * 512], in_=p[0][:, 0:512], func=AF.Silu), deps=[mmgu])
            else:
                sev = C("vector", lambda e, fc=fc: e.tensor_copy(
                    out=SUb[:, (fc - 6) * 512:(fc - 5) * 512], in_=p[0][:, 0:512]), deps=[mmgu])
            prevc[("sev", fc)] = sev
            hs_done.append(sev)
        hmul = []
        for fc in range(6):
            hh = C("vector", lambda e, fc=fc: e.tensor_tensor(
                HSh[:, fc * 512:(fc + 1) * 512], SGb[:, fc * 512:(fc + 1) * 512],
                SUb[:, fc * 512:(fc + 1) * 512], ALU.mult),
                deps=[hs_done[fc], hs_done[6 + fc], prevc.get(("hsh", fc))])
            prevc[("hsh", fc)] = hh
            hmul.append(hh)
        prevc["gu_last"] = hmul[-1]
        for i in range(4):
            gi = 4 * g + i
            t0 = gi * 128
            par = gi % 2
            ypar = gi % 3
            dp2 = prevc
            pr1 = p[4] if par == 0 else p[2]
            pr2 = p[5] if par == 0 else p[3]
            for tpre in ([0, 1, 2] if gi == 0 else ([gi + 2] if gi + 2 < NT else [])):
                mzpre[tpre] = C("scalar", lambda e, yp=tpre % 3: e.memzero(Yg[yp][:]),
                                deps=[dp2.get(("ygwar4", tpre % 3))]
                                     + (b_final if tpre < 3 else []))
            mzy = mzpre[gi]
            yg_g = []
            for jj in range(8):
                yg_g.append(DMA("gpsimd", lambda e, jj=jj, gi=gi, ypar=ypar: e.indirect_dma_start(
                    out=Yg[ypar][:, jj * 1024:(jj + 1) * 1024],
                    out_offset=None,
                    in_=YAG[:, :], in_offset=IndirectOffsetOnAxis(S8ALL[:, gi * 8 + jj:gi * 8 + jj + 1], 0),
                    bounds_check=rB32K, oob_is_err=False),
                    deps=[mzy, s8_done[0], s8_done[1], cc_y, lws2]))
            def rd_mm(e, ypar=ypar, pr1=pr1, i=i):
                for fc in range(6):
                    e.matmul(pr1[:, 0:512], HSh[:, fc * 512 + i * 128: fc * 512 + (i + 1) * 128],
                             wsd[:, fc * 1024: fc * 1024 + 512], start=(fc == 0), stop=False)
                for jj in range(8):
                    ins = e.matmul(pr1[:, 0:512], cID[:], Yg[ypar][:, jj * 1024: jj * 1024 + 512],
                                   start=False, stop=(jj == 7))
                return ins
            def rd_mm2(e, ypar=ypar, pr2=pr2, i=i):
                for fc in range(6):
                    e.matmul(pr2[:, 0:512], HSh[:, fc * 512 + i * 128: fc * 512 + (i + 1) * 128],
                             wsd[:, fc * 1024 + 512: (fc + 1) * 1024], start=(fc == 0), stop=False)
                for jj in range(8):
                    ins = e.matmul(pr2[:, 0:512], cID[:], Yg[ypar][:, jj * 1024 + 512: (jj + 1) * 1024],
                                   start=False, stop=(jj == 7))
                return ins
            rdeps = yg_g + [ld_cid, lws2, dp2.get(("c1war", par))] + hmul
            if par == 1:
                rdeps = rdeps + hs_done
            mr1 = C("tensor", rd_mm, deps=rdeps)
            mr2 = C("tensor", rd_mm2, deps=rdeps + [mr1])
            dp2[("ygwar4", ypar)] = mr2
            ldx1b = DMA("sync", lambda e, t0=t0, par=par: e.dma_start(
                out=X1B2[par][:], in_=X1TOK[t0:t0 + 128, :]),
                deps=[prev_tile["x1wr"], dp2.get(("x1b2war", par))] + (b_final if gi == 0 else []))
            c2a = C("vector", lambda e, par=par, pr1=pr1: e.tensor_tensor(
                C1b[par][:, 0:512], X1B2[par][:, 0:512], pr1[:, 0:512], ALU.add), deps=[mr1, ldx1b])
            c2 = C("vector", lambda e, par=par, pr2=pr2: e.tensor_tensor(
                C1b[par][:, 512:1024], X1B2[par][:, 512:1024], pr2[:, 0:512], ALU.add),
                deps=[mr2, ldx1b, c2a])
            dp2[("x1b2war", par)] = c2
            dp2[("c1war", par)] = c2
            if par == 1:
                dp2["odd_rout"] = c2
            vnm = C("vector", lambda e, par=par: e.tensor_reduce(
                CL2[par][:, 0:1], C1b[par][:], AX.X, ALU.add, negate=True), deps=[c2])
            vnm2 = C("vector", lambda e, par=par: e.tensor_scalar_mul(
                CL2[par][:, 0:1], CL2[par][:, 0:1], 1.0 / D), deps=[vnm])
            ssq = C("scalar", lambda e, par=par: e.activation(
                out=SQ2[par][:], in_=C1b[par][:], func=AF.Square), deps=[c2, dp2.get(("sq2war", par))])
            vsq = C("vector", lambda e, par=par: e.reduce_sum(
                CL2[par][:, 1:2], SQ2[par][:], axis=AX.X), deps=[ssq])
            dp2[("sq2war", par)] = vsq
            vm2 = C("vector", lambda e, par=par: e.tensor_tensor(
                CL2[par][:, 2:3], CL2[par][:, 0:1], CL2[par][:, 0:1], ALU.mult), deps=[vnm2])
            vvar = C("vector", lambda e, par=par: e.tensor_scalar(
                CL2[par][:, 1:2], CL2[par][:, 1:2], 1.0 / D, 1e-5, ALU.mult, ALU.add), deps=[vsq])
            vvar2 = C("vector", lambda e, par=par: e.tensor_sub(
                CL2[par][:, 1:2], CL2[par][:, 1:2], CL2[par][:, 2:3]), deps=[vvar, vm2])
            ssd = C("scalar", lambda e, par=par: e.activation(
                out=CL2[par][:, 2:3], in_=CL2[par][:, 1:2], func=AF.Sqrt), deps=[vvar2])
            vrsd = C("vector", lambda e, par=par: e.reciprocal(
                CL2[par][:, 3:4], CL2[par][:, 2:3]), deps=[ssd])
            vout = C("vector", lambda e, par=par: e.scalar_tensor_tensor(
                OUTt[par][:], C1b[par][:], CL2[par][:, 0:1],
                CL2[par][:, 3:4].broadcast_to([128, 1024]), ALU.add, ALU.mult),
                deps=[vrsd, dp2.get(("outwar", par))])
            wout = DMA("sync", lambda e, t0=t0, par=par: e.dma_start(
                out=outs["OUT"][t0:t0 + 128, :], in_=OUTt[par][:]), deps=[vout])
            dp2[("outwar", par)] = wout
            out_wr.append(wout)
    S("sync", lambda e: None, inc=0, deps=out_wr[-1:] + out_wr[:1])
    return nc, P


_NC_CACHE = {}

def _build_cached():
    if "nc" not in _NC_CACHE:
        nc, P = build(stage="full")
        with nc.Block() as block:
            P.build(block)
        _NC_CACHE["nc"] = nc
    return _NC_CACHE["nc"]


def kernel(L0_attn_hidden_states, L0_attn_Wq, L0_attn_Wk, L0_attn_Wv,
           L0_attn_Wg1, L0_attn_Wg2, L0_attn_g_norm_weight, L0_attn_Wo,
           L0_ffn_router_w, L0_ffn_w_gate, L0_ffn_w_up, L0_ffn_w_down,
           L0_ffn_ws_gate_up, L0_ffn_ws_down):
    x = np.ascontiguousarray(np.asarray(L0_attn_hidden_states, dtype=np.float32))
    Wq = np.asarray(L0_attn_Wq, np.float32); Wk = np.asarray(L0_attn_Wk, np.float32)
    Wv = np.asarray(L0_attn_Wv, np.float32); Wo = np.asarray(L0_attn_Wo, np.float32)
    Wg1 = np.asarray(L0_attn_Wg1, np.float32); Wg2 = np.asarray(L0_attn_Wg2, np.float32)
    gnw = np.asarray(L0_attn_g_norm_weight, np.float32)
    rw = np.asarray(L0_ffn_router_w, np.float32)
    wge = np.asarray(L0_ffn_w_gate, np.float32); wue = np.asarray(L0_ffn_w_up, np.float32)
    wde = np.asarray(L0_ffn_w_down, np.float32)
    wsgu = np.ascontiguousarray(np.asarray(L0_ffn_ws_gate_up, np.float32))
    wsd = np.ascontiguousarray(np.asarray(L0_ffn_ws_down, np.float32))

    WQT = np.ascontiguousarray(Wq.T); WKT = np.ascontiguousarray(Wk.T)
    WVT = np.ascontiguousarray(Wv.T); WOT = np.ascontiguousarray(Wo.T)
    WG1T = np.ascontiguousarray(Wg1.T); WG2T = np.ascontiguousarray(Wg2.T)
    RWT = np.ascontiguousarray(rw.T)
    GNW = np.ascontiguousarray(gnw.reshape(8, 128).T)
    r = np.arange(128)
    UTm = (r[:, None] <= r[None, :]).astype(np.float32)
    IDm = np.eye(128, dtype=np.float32)
    IOTAEm = np.broadcast_to((np.arange(64) * 512 + 1).astype(np.float32), (128, 64)).copy()
    EXP48m = (np.arange(32)[None, :] // 4 == np.arange(8)[:, None]).astype(np.float32)
    PVALm = ((np.arange(32) % 4)[:, None] * 128 + np.arange(128)[None, :]).astype(np.float32)

    in_maps = []
    for c in range(8):
        d = {
            "XT": np.ascontiguousarray(x[c].T), "XTOK": np.ascontiguousarray(x[c]),
            "WQT": WQT, "WKT": WKT, "WVT": WVT, "WOT": WOT,
            "WG1T": WG1T, "WG2T": WG2T, "GNW": GNW, "RWT": RWT,
            "WSGU": wsgu, "WSD": wsd,
            "WGE": np.ascontiguousarray(wge[8 * c:8 * c + 8]),
            "WUE": np.ascontiguousarray(wue[8 * c:8 * c + 8]),
            "WDE": np.ascontiguousarray(wde[8 * c:8 * c + 8]),
            "UT": UTm, "MST": UTm, "IDT": IDm, "IOTAE": IOTAEm,
            "TOKID": (4096 * c + 128 * np.arange(32)[None, :] + r[:, None] + 1).astype(np.float32),
            "PREM": (np.arange(8)[:, None] < c).astype(np.float32),
            "SEL64": (np.arange(64)[:, None] == (8 * c + np.arange(8))[None, :]).astype(np.float32),
            "EXP48": EXP48m, "PVAL": PVALm,
            "EOFF": ((8 * c + np.arange(32) // 4) * 512).astype(np.float32)[:, None],
            "ECON": np.broadcast_to((8 * c + np.arange(32) // 4).astype(np.float32), (128, 32)).copy(),
        }
        in_maps.append(d)

    nc = _build_cached()
    res = run_bass_kernel_spmd(nc, in_maps, core_ids=list(range(8)))
    return np.stack([np.asarray(res.results[c]["OUT"]) for c in range(8)], 0)
